# version 01
# baseline (speedup 1.0000x reference)
import numpy as np
from contextlib import ExitStack

import concourse.bass as bass
import concourse.mybir as mybir
from concourse.bass_utils import run_bass_kernel_spmd

F32 = mybir.dt.float32
BF16 = mybir.dt.bfloat16
I32 = mybir.dt.int32
ALU = mybir.AluOpType
AF = mybir.ActivationFunctionType
AX = mybir.AxisListType.X

P = 128
D = 4096
HID = 11008
NSLAB = 86
FT = 32
T = 512
NCHUNK = 4
TOK_CORE = T * NCHUNK
NCORE = 8
WIN = 64
RSQ = float(1.0 / np.sqrt(128.0))
EPS = 1e-5
QN = [22, 22, 21, 21]
QS0 = [0, 22, 44, 65]
OVN0 = 12288


class Sched:
    ENGINES = ("sync", "scalar", "vector", "tensor", "gpsimd")

    def __init__(self):
        self.ops = {e: [] for e in self.ENGINES}
        self.cnt = {}

    ENG_SEM = {"vector": "cDVE", "scalar": "cACT", "gpsimd": "cPOOL"}

    def emit(self, eng, fn, waits=(), inc=None, amt=1, dma=False):
        ws = []
        if not dma and eng in self.ENG_SEM:
            osem = self.ENG_SEM[eng]
            prev = self.cnt.get(osem, 0)
            waits = list(waits) + [(osem, prev)]
            assert inc is None or inc == osem, f"{eng} compute op inc={inc}"
            inc, amt = osem, 1
        for sm, v in waits:
            if v > 0:
                assert v <= self.cnt.get(sm, 0), \
                    f"wait on {sm}>={v} but only {self.cnt.get(sm, 0)} emitted"
                ws.append((sm, v))
        self.ops[eng].append((ws, fn, inc, amt))
        if inc is not None:
            self.cnt[inc] = self.cnt.get(inc, 0) + amt
        return self.cnt.get(inc, 0) if inc else None

    def c(self, sem):
        return self.cnt.get(sem, 0)


def build_nc(nchunk=NCHUNK, dbg=False, upto=99, cupto=99):
    nc = bass.Bass("TRN2", target_bir_lowering=False)
    tok_core = T * nchunk

    x_in = nc.dram_tensor("x_in", [tok_core, D], F32, kind="ExternalInput").ap()
    wq = nc.dram_tensor("wq", [D, 4096], F32, kind="ExternalInput").ap()
    wk = nc.dram_tensor("wk", [D, 1024], F32, kind="ExternalInput").ap()
    wv = nc.dram_tensor("wv", [D, 1024], F32, kind="ExternalInput").ap()
    bq = nc.dram_tensor("bq", [P, 32], F32, kind="ExternalInput").ap()
    bk = nc.dram_tensor("bk", [P, 8], F32, kind="ExternalInput").ap()
    bv = nc.dram_tensor("bv", [P, 8], F32, kind="ExternalInput").ap()
    wg = nc.dram_tensor("wg", [HID, D], I32, kind="ExternalInput").ap()
    wu = nc.dram_tensor("wu", [HID, D], I32, kind="ExternalInput").ap()
    wd = nc.dram_tensor("wd", [D, HID], I32, kind="ExternalInput").ap()
    scl = nc.dram_tensor("scl", [P, 3], F32, kind="ExternalInput").ap()
    eyef = nc.dram_tensor("eyef", [P, P], F32, kind="ExternalInput").ap()
    eyeb = nc.dram_tensor("eyeb", [P, P], BF16, kind="ExternalInput").ap()
    onesb = nc.dram_tensor("onesb", [P, 1], BF16, kind="ExternalInput").ap()
    onesf = nc.dram_tensor("onesf", [1, P], F32, kind="ExternalInput").ap()
    y_out = nc.dram_tensor("y_out", [tok_core, D], F32, kind="ExternalOutput").ap()

    wattn_b = nc.dram_tensor("wattn_b", [D, 3072], BF16, kind="Internal").ap()
    wg_b = nc.dram_tensor("wg_b", [HID, D], BF16, kind="Internal").ap()
    wu_b = nc.dram_tensor("wu_b", [HID, D], BF16, kind="Internal").ap()
    wd_b = nc.dram_tensor("wd_b", [D, HID], BF16, kind="Internal").ap()
    wgT_b = nc.dram_tensor("wgT_b", [D, HID], BF16, kind="Internal").ap()
    wuT_b = nc.dram_tensor("wuT_b", [D, HID], BF16, kind="Internal").ap()
    wdT_b = nc.dram_tensor("wdT_b", [HID, D], BF16, kind="Internal").ap()

    dbg_t = {}
    if dbg:
        for name in ("d_x", "d_qkv", "d_ovn", "d_h1", "d_h2"):
            dbg_t[name] = nc.dram_tensor(name, [P, 16384], BF16, kind="ExternalOutput").ap()
        for name in ("d_bqkt", "d_ebuf", "d_bet", "d_bpack"):
            dbg_t[name] = nc.dram_tensor(name, [P, 4672], BF16, kind="ExternalOutput").ap()
        dbg_t["d_pov"] = nc.dram_tensor("d_pov", [P, 512], F32, kind="ExternalOutput").ap()

    s = Sched()

    with ExitStack() as ctx:
        en = ctx.enter_context
        RES = en(nc.sbuf_tensor("RES", [P, FT * T], BF16))
        QKVO = en(nc.sbuf_tensor("QKVO", [P, 16384], BF16))
        WBUF = en(nc.sbuf_tensor("WBUF", [P, 16384], BF16))
        MIDQ = en(nc.sbuf_tensor("MIDQ", [P, 11264], BF16))
        PART = en(nc.sbuf_tensor("PART", [P, 16384], BF16))
        BPACK = en(nc.sbuf_tensor("BPACK", [P, 1600], BF16))
        OSTG = en(nc.sbuf_tensor("OSTG", [P, 8192], BF16))
        WATT = en(nc.sbuf_tensor("WATT", [P, 1024], BF16))
        EBUF = en(nc.sbuf_tensor("EBUF", [P, 1024], BF16))
        BET = en(nc.sbuf_tensor("BET", [P, 1024], BF16))
        BQKT = en(nc.sbuf_tensor("BQKT", [P, 2048], BF16))
        GSIL = en(nc.sbuf_tensor("GSIL", [P, 1024], F32))
        NM = en(nc.sbuf_tensor("NM", [P, 2], F32))
        RW = en(nc.sbuf_tensor("RW", [P, 64], F32))
        SVEC = en(nc.sbuf_tensor("SVEC", [1, 3072], F32))
        CEYEF = en(nc.sbuf_tensor("CEYEF", [P, P], F32))
        CEYEB = en(nc.sbuf_tensor("CEYEB", [P, P], BF16))
        CONESB = en(nc.sbuf_tensor("CONESB", [P, 1], BF16))
        CONESF = en(nc.sbuf_tensor("CONESF", [1, P], F32))
        CBQ = en(nc.sbuf_tensor("CBQ", [P, 48], F32))
        CBK = en(nc.sbuf_tensor("CBK", [P, 8], F32))
        CBV = en(nc.sbuf_tensor("CBV", [P, 8], F32))
        CSCL = en(nc.sbuf_tensor("CSCL", [P, 3], F32))
        CBQS = en(nc.sbuf_tensor("CBQS", [P, 8], F32))

        PS = [en(nc.psum_tensor(f"PS{i}", [P, 512], F32)) for i in range(4)]
        PTB = [en(nc.psum_tensor(f"PTB{i}", [P, 1024], BF16)) for i in range(4)]

        sem_names = ["dX0", "dX1", "dWA", "dWA0", "dWA1", "dWFg", "dWFu",
                     "dWFgu", "dWFwd", "dWD0", "dWD1", "dCast", "dRawG",
                     "dRawU", "dPWA", "dPWBg", "dPWBu", "dPWCgu", "dPWCwd",
                     "dST", "dMisc", "cPE", "cACT", "cDVE", "cPOOL"]
        sems = {n: en(nc.semaphore(n)) for n in sem_names}
        block = en(nc.Block())

        xstage = WBUF[:, :].bitcast(F32)
        partf = PART[:, :].bitcast(F32)
        h2f = PART[:, :].bitcast(F32)
        ostf = OSTG[:, :].bitcast(F32)
        WDN = OSTG

        def res_t(h):
            return RES[:, T * h: T * h + T]

        for dst, src in [(CEYEF[:, :], eyef), (CEYEB[:, :], eyeb),
                         (CONESB[:, :], onesb), (CONESF[:, :], onesf),
                         (CBQ[:, 0:32], bq), (CBK[:, :], bk), (CBV[:, :], bv),
                         (CSCL[:, :], scl)]:
            s.emit("sync", lambda sy, d=dst, sr=src: sy.dma_start(out=d, in_=sr),
                   inc="dMisc", amt=16, dma=True)
        w_misc = s.c("dMisc")

        s.emit("vector", lambda v: v.tensor_add(
            CBQ[:, 32:48], CBQ[:, 0:32:2], CBQ[:, 1:32:2]),
            waits=[("dMisc", w_misc)])
        s.emit("vector", lambda v: v.tensor_add(
            CBQ[:, 0:8], CBQ[:, 32:48:2], CBQ[:, 33:48:2]))
        s.emit("vector", lambda v: v.tensor_scalar_mul(CBQS[:, :], CBQ[:, 0:8], RSQ))
        s.emit("vector", lambda v: v.memset(BPACK[:, 0:1600], 0.0))
        s.emit("vector", lambda v: v.memset(
            BPACK[:, 1024:1600].rearrange("p (j c) -> p j c", c=9)[:, :, 8], 1.0))

        wq_src = wq.rearrange("(d h) c -> h d c", h=32)
        wk_src = wk.rearrange("(d h) c -> h d c", h=32)
        wv_src = wv.rearrange("(d h) c -> h d c", h=32)
        for h in range(FT):
            wprev = s.c("cDVE")
            s.emit("sync", lambda sy, h=h: sy.dma_start(out=partf[:, 0:4096], in_=wq_src[h, :, :]),
                   waits=[("cDVE", wprev)], inc="dWA", amt=16, dma=True)
            s.emit("sync", lambda sy, h=h: sy.dma_start(out=partf[:, 4096:5120], in_=wk_src[h, :, :]),
                   inc="dWA", amt=16, dma=True)
            s.emit("sync", lambda sy, h=h: sy.dma_start(out=partf[:, 5120:6144], in_=wv_src[h, :, :]),
                   inc="dWA", amt=16, dma=True)
            wload = s.c("dWA")
            wpw = s.c("dPWA")

            s.emit("vector", lambda v: v.tensor_add(
                partf[:, 6144:8192], partf[:, 0:4096:2], partf[:, 1:4096:2]),
                waits=[("dWA", wload), ("dPWA", wpw)])
            s.emit("vector", lambda v: v.tensor_add(
                partf[:, 0:1024], partf[:, 6144:8192:2], partf[:, 6145:8192:2]))
            s.emit("vector", lambda v: v.tensor_scalar(
                out=MIDQ[:, 0:1024].rearrange("p (g d) -> p g d", d=P),
                in0=partf[:, 0:1024].rearrange("p (d g) -> p g d", g=8),
                scalar1=RSQ, scalar2=None, op0=ALU.mult))
            s.emit("vector", lambda v: v.tensor_copy(
                out=MIDQ[:, 1024:2048].rearrange("p (g e) -> p g e", e=P),
                in_=partf[:, 4096:5120].rearrange("p (e g) -> p g e", g=8)))
            s.emit("vector", lambda v: v.tensor_copy(
                out=MIDQ[:, 2048:3072].rearrange("p (g e) -> p g e", e=P),
                in_=partf[:, 5120:6144].rearrange("p (e g) -> p g e", g=8)))
            wprep = s.c("cDVE")
            s.emit("scalar", lambda sc, h=h: sc.dma_start(
                out=wattn_b[P * h: P * h + P, :], in_=MIDQ[:, 0:3072]),
                waits=[("cDVE", wprep)], inc="dPWA", amt=16, dma=True)
        p1_pw = s.c("dPWA")
        p1_dve = s.c("cDVE")

        wd_src = wd.rearrange("(d h) c -> h d c", h=32)
        wd_dst = wd_b.rearrange("(h d) c -> h d c", d=P)
        HH = HID // 2
        s.emit("gpsimd", lambda g: g.dma_start(out=wd_dst[:, :, 0:HH], in_=wd_src[:, :, 0:HH]),
               inc="dCast", amt=16, dma=True)
        s.emit("gpsimd", lambda g: g.dma_start(out=wd_dst[:, :, HH:HID], in_=wd_src[:, :, HH:HID]),
               inc="dCast", amt=16, dma=True)
        wd_cast_done = s.c("dCast")

        perm_done = {}
        chains = {
            "g": (wg, wg_b, MIDQ[:, 3072:7168], MIDQ[:, 7168:11264], "dRawG", "dPWBg"),
            "u": (wu, wu_b, PART[:, 8192:12288], PART[:, 12288:16384], "dRawU", "dPWBu"),
        }
        for sl in range(NSLAB):
            for mat in ("g", "u"):
                msrc, mdst, sin, sout, rsem, psem = chains[mat]
                wpool = s.c("cPOOL")
                wpw = s.c(psem)
                first = (sl == 0)
                waits = [("cPOOL", max(0, wpool - 1))]
                if first:
                    waits.append(("cDVE", p1_dve))
                s.emit("gpsimd", lambda g, sr=msrc, sl=sl, sin=sin: g.dma_start(
                    out=sin, in_=sr[P * sl:P * sl + P, :]),
                    waits=waits, inc=rsem, amt=16, dma=True)
                wload = s.c(rsem)
                s.emit("gpsimd", lambda g, sin=sin, sout=sout: g.tensor_copy(
                    out=sout.rearrange("p (h d) -> p h d", d=P),
                    in_=sin.rearrange("p (d h) -> p h d", h=32)),
                    waits=[(rsem, wload), (psem, wpw)])
                wperm = s.c("cPOOL")
                s.emit("scalar", lambda sc, dd=mdst, sl=sl, sout=sout: sc.dma_start(
                    out=dd[P * sl:P * sl + P, :], in_=sout),
                    waits=[("cPOOL", wperm)], inc=psem, amt=16, dma=True)
                perm_done[(mat, sl)] = s.c(psem)

        QHB = [QN[i] * P for i in range(4)]
        QH0 = [QS0[i] * P for i in range(4)]
        gT_done = {}
        dT_done = {}
        for q in range(4):
            for mat, mb, mTb in (("g", wg_b, wgT_b), ("u", wu_b, wuT_b)):
                need = perm_done[(mat, QS0[q] + QN[q] - 1)]
                for h in range(FT):
                    wpw = s.c("dPWCgu")
                    waits = [(chains[mat][5], need), ("dPWCgu", wpw)]
                    if q == 0 and h == 0 and mat == "g":
                        waits.append(("cDVE", p1_dve))
                    s.emit("sync", lambda sy, mb=mb, q=q, h=h: sy.dma_start(
                        out=PART[:, 0:QHB[q]],
                        in_=mb[QH0[q]:QH0[q] + QHB[q], P * h:P * h + P],
                        transpose=True),
                        waits=waits, inc="dWFgu", amt=16, dma=True)
                    wrd = s.c("dWFgu")
                    s.emit("scalar", lambda sc, mTb=mTb, q=q, h=h: sc.dma_start(
                        out=mTb[P * h:P * h + P, QH0[q]:QH0[q] + QHB[q]],
                        in_=PART[:, 0:QHB[q]]),
                        waits=[("dWFgu", wrd)], inc="dPWCgu", amt=16, dma=True)
                gT_done[(mat, q)] = s.c("dPWCgu")
            for sl in range(QS0[q], QS0[q] + QN[q]):
                wpw = s.c("dPWCwd")
                waits = [("dCast", wd_cast_done), ("dPWCwd", wpw)]
                if sl == 0:
                    waits.append(("cDVE", p1_dve))
                s.emit("sync", lambda sy, sl=sl: sy.dma_start(
                    out=PART[:, 2816:6912],
                    in_=wd_b[0:D, P * sl:P * sl + P], transpose=True),
                    waits=waits, inc="dWFwd", amt=16, dma=True)
                wrd = s.c("dWFwd")
                s.emit("scalar", lambda sc, sl=sl: sc.dma_start(
                    out=wdT_b[P * sl:P * sl + P, :], in_=PART[:, 2816:6912]),
                    waits=[("dWFwd", wrd)], inc="dPWCwd", amt=16, dma=True)
                dT_done[sl] = s.c("dPWCwd")
        p3_pw_gu = s.c("dPWCgu")
        p3_pw_wd = s.c("dPWCwd")

        st = dict(
            store_done=0,
            xslot_free=[0, 0],
            res_ready=0,
            qkv_evac=0,
            wbuf_free=0,
            qkvo_free_pe=0,
            merge_done=0, ln1_done=0,
            sil_free=[0, 0],
            mid_free=[0, 0],
            wdn_free=[0, 0],
            dn_free=0,
            out_tr=0,
        )
        mid = dict(mm1=[0, 0], exp=[0, 0], nmx=[0, 0], etr=[0, 0], etev=[0, 0],
                   mm3=[0, 0], bqkev=[0, 0], btr=[0, 0], norm={2: 0, 3: 0}, bld=0)

        def phase_A(c):
            for tt in range(4):
                par = tt % 2
                slot = xstage[:, 4096 * par: 4096 * par + 4096]
                dxs = "dX0" if par == 0 else "dX1"
                s.emit("sync", lambda sy, c=c, tt=tt, sl=slot: sy.dma_start(
                    out=sl, in_=x_in[T * c + P * tt: T * c + P * (tt + 1), :]),
                    waits=[("cPE", st["xslot_free"][par]), ("dST", st["store_done"])],
                    inc=dxs, amt=16, dma=True)
                wx = s.c(dxs)
                for h in range(FT):
                    bank = PS[2 + (h % 2)]
                    wev = s.c("cACT")
                    s.emit("tensor", lambda t, sl=slot, h=h, b=bank, dxs=dxs: t.transpose(
                        b[:, 0:P], sl[:, h:4096:32], CEYEF[:, :]),
                        waits=[(dxs, wx), ("cACT", wev - 1)], inc="cPE")
                    wtr = s.c("cPE")
                    s.emit("scalar", lambda sc, h=h, tt=tt, b=bank: sc.copy(
                        out=res_t(h)[:, P * tt:P * (tt + 1)], in_=b[:, 0:P]),
                        waits=[("cPE", wtr)], inc="cACT")
                st["xslot_free"][par] = s.c("cPE")
            st["res_ready"] = s.c("cACT")
            st["wbuf_free"] = s.c("cPE")

        def phase_B(c):
            bank_free_dve = s.c("cDVE")
            for grp in range(6):
                grp_mm_base = s.c("cPE")
                for h in range(FT):
                    par = h % 2
                    ws = WATT[:, 512 * par: 512 * par + 512]
                    wfree = grp_mm_base + h - 1 if h >= 2 else grp_mm_base
                    was = "dWA0" if par == 0 else "dWA1"
                    s.emit("sync", lambda sy, grp=grp, h=h, ws=ws: sy.dma_start(
                        out=ws, in_=wattn_b[P * h: P * h + P, 512 * grp: 512 * grp + 512]),
                        waits=[("cPE", wfree), ("dPWA", p1_pw)], inc=was, amt=16, dma=True)
                    wld = s.c(was)
                    for m6 in range(4):
                        waits = [(was, wld)]
                        if h == 0:
                            waits += [("cACT", st["res_ready"]), ("cDVE", bank_free_dve)]
                        s.emit("tensor", lambda t, m6=m6, h=h, ws=ws: t.matmul(
                            PS[m6][:, 0:T], ws[:, 128 * m6:128 * m6 + 128], res_t(h),
                            start=(h == 0), stop=(h == FT - 1)),
                            waits=waits, inc=("cPE" if m6 == 3 else None))
                wmm = s.c("cPE")
                for m6 in range(4):
                    m = grp * 4 + m6
                    if m < 8:
                        bias = CBQS[:, m:m + 1]
                    elif m < 16:
                        bias = CBK[:, m - 8:m - 7]
                    else:
                        bias = CBV[:, m - 16:m - 15]
                    s.emit("vector", lambda v, m=m, m6=m6, bias=bias: v.tensor_scalar(
                        out=QKVO[:, T * m: T * m + T], in0=PS[m6][:, 0:T],
                        scalar1=bias, scalar2=None, op0=ALU.add),
                        waits=[("cPE", wmm)], inc="cDVE")
                bank_free_dve = s.c("cDVE")
            st["qkv_evac"] = s.c("cDVE")

        def phase_C(c):
            for w in range(T // WIN):
                w0 = WIN * w
                bw = [("cDVE", st["qkv_evac"]),
                      ("cPE", max(mid["btr"][0], mid["btr"][1],
                                  mid["mm3"][0], mid["mm3"][1])),
                      ("dST", st["store_done"])]
                for g in range(8):
                    s.emit("vector", lambda v, g=g, w0=w0: v.tensor_copy(
                        out=BPACK[:, 0:512].rearrange("p (j gg) -> p j gg", gg=8)[:, :, g],
                        in_=QKVO[:, T * g + w0: T * g + w0 + WIN]),
                        waits=(bw if g == 0 else []))
                for g in range(8):
                    s.emit("vector", lambda v, g=g, w0=w0: v.tensor_copy(
                        out=BPACK[:, 512:1024].rearrange("p (j gg) -> p j gg", gg=8)[:, :, g],
                        in_=QKVO[:, T * (8 + g) + w0: T * (8 + g) + w0 + WIN]))
                for g in range(8):
                    s.emit("vector", lambda v, g=g, w0=w0: v.tensor_copy(
                        out=BPACK[:, 1024:1600].rearrange("p (j cc) -> p j cc", cc=9)[:, :, g],
                        in_=QKVO[:, T * (16 + g) + w0: T * (16 + g) + w0 + WIN]))
                mid["bld"] = s.c("cDVE")

                for j in range(16):
                    if cupto < 2:
                        continue
                    pj = j % 2
                    ovbank = 2 + (1 if j >= 8 else 0)
                    for t4 in range(4):
                        tok = 4 * j + t4
                        s.emit("tensor", lambda t, pj=pj, t4=t4, tok=tok: t.transpose(
                            PTB[pj][0:8, 256 * t4:256 * t4 + 128],
                            BPACK[:, 8 * tok:8 * tok + 8], CEYEB[:, :]),
                            waits=([("cDVE", max(mid["bld"], mid["bqkev"][pj]))]
                                   if t4 == 0 else []))
                        s.emit("tensor", lambda t, pj=pj, t4=t4, tok=tok: t.transpose(
                            PTB[pj][0:8, 256 * t4 + 128:256 * t4 + 256],
                            BPACK[:, 512 + 8 * tok:512 + 8 * tok + 8], CEYEB[:, :]),
                            inc=("cPE" if t4 == 3 else None))
                    mid["btr"][pj] = s.c("cPE")
                    if cupto < 3:
                        continue
                    s.emit("vector", lambda v, pj=pj: v.tensor_copy(
                        out=BQKT[0:8, 1024 * pj:1024 * pj + 1024],
                        in_=PTB[pj][0:8, 0:1024]),
                        waits=[("cPE", mid["btr"][pj]), ("cPE", mid["mm1"][pj]),
                               ("dST", st["store_done"])],
                        inc="cDVE")
                    mid["bqkev"][pj] = s.c("cDVE")
                    if cupto < 4:
                        continue
                    for t4 in range(4):
                        s.emit("tensor", lambda t, pj=pj, t4=t4: t.matmul(
                            PS[pj][:, 128 * t4:128 * t4 + 128],
                            BQKT[0:8, 1024 * pj + 256 * t4:1024 * pj + 256 * t4 + 128],
                            BQKT[0:8, 1024 * pj + 256 * t4 + 128:1024 * pj + 256 * t4 + 256],
                            start=True, stop=True),
                            waits=([("cDVE", max(mid["bqkev"][pj], mid["nmx"][pj])),
                                    ("cACT", mid["exp"][pj])] if t4 == 0 else []),
                            inc=("cPE" if t4 == 3 else None))
                    mid["mm1"][pj] = s.c("cPE")
                    if cupto < 5:
                        continue
                    s.emit("vector", lambda v, pj=pj: v.reduce_max(
                        out=NM[:, pj:pj + 1], in_=PS[pj][:, 0:512], axis=AX, negate=True),
                        waits=[("cPE", mid["mm1"][pj])], inc="cDVE")
                    mid["nmx"][pj] = s.c("cDVE")
                    if cupto < 6:
                        continue
                    s.emit("scalar", lambda sc, pj=pj: sc.activation(
                        out=EBUF[:, 512 * pj:512 * pj + 512], in_=PS[pj][:, 0:512],
                        func=AF.Exp, bias=NM[:, pj:pj + 1]),
                        waits=[("cDVE", mid["nmx"][pj]), ("cPE", mid["etr"][pj]),
                               ("dST", st["store_done"])],
                        inc="cACT")
                    mid["exp"][pj] = s.c("cACT")
                    if cupto < 7:
                        continue
                    for t4 in range(4):
                        s.emit("tensor", lambda t, pj=pj, t4=t4: t.transpose(
                            PTB[2 + pj][:, 128 * t4:128 * t4 + 128],
                            EBUF[:, 512 * pj + 128 * t4:512 * pj + 128 * t4 + 128],
                            CEYEB[:, :]),
                            waits=([("cACT", max(mid["exp"][pj], mid["etev"][pj]))]
                                   if t4 == 0 else []),
                            inc=("cPE" if t4 == 3 else None))
                    mid["etr"][pj] = s.c("cPE")
                    s.emit("scalar", lambda sc, pj=pj: sc.copy(
                        out=BET[:, 512 * pj:512 * pj + 512],
                        in_=PTB[2 + pj][:, 0:512]),
                        waits=[("cPE", max(mid["etr"][pj], mid["mm3"][pj])),
                               ("dST", st["store_done"])],
                        inc="cACT")
                    mid["etev"][pj] = s.c("cACT")
                    if cupto < 8:
                        continue
                    for t4 in range(4):
                        tl = (4 * j + t4) % 32
                        s.emit("tensor", lambda t, pj=pj, t4=t4, tl=tl, ob=ovbank, j=j: t.matmul(
                            PS[ob][:, 16 * tl:16 * tl + 9],
                            BET[:, 512 * pj + 128 * t4:512 * pj + 128 * t4 + 128],
                            BPACK[:, 1024 + 9 * (4 * j + t4):1024 + 9 * (4 * j + t4) + 9],
                            start=True, stop=True),
                            waits=([("cACT", mid["etev"][pj]),
                                    ("cDVE", mid["norm"][ovbank])] if t4 == 0 else []),
                            inc=("cPE" if t4 == 3 else None))
                    mid["mm3"][pj] = s.c("cPE")
                    if cupto < 9:
                        continue
                    if j % 8 == 7:
                        hb = j // 8
                        ob = 2 + hb
                        tok0 = w0 + 32 * hb
                        wmm3 = s.c("cPE")
                        s.emit("vector", lambda v, hb=hb, ob=ob: v.reciprocal(
                            out=RW[:, 32 * hb:32 * hb + 32], in_=PS[ob][:, 8:512:16]),
                            waits=[("cPE", wmm3)], inc="cDVE")
                        wrec = s.c("cDVE")
                        for g in range(8):
                            s.emit("vector", lambda v, g=g, hb=hb, ob=ob, tok0=tok0:
                                   v.tensor_tensor(
                                       out=QKVO[:, OVN0 + 8 * tok0 + g:
                                                OVN0 + 8 * tok0 + g + 8 * 31 + 1:8],
                                       in0=PS[ob][:, g:512:16],
                                       in1=RW[:, 32 * hb:32 * hb + 32], op=ALU.mult),
                                   waits=([("cDVE", wrec)] if g == 0 else []),
                                   inc=("cDVE" if g == 7 else None))
                        mid["norm"][ob] = s.c("cDVE")

        def phase_D(c):
            wnorm = max(mid["norm"][2], mid["norm"][3])
            for h in range(FT):
                g = h >> 2
                s.emit("vector", lambda v, h=h, g=g: v.tensor_add(
                    res_t(h), res_t(h), QKVO[:, OVN0 + g:OVN0 + g + 8 * (T - 1) + 1:8]),
                    waits=([("cDVE", wnorm), ("dST", st["store_done"])] if h == 0 else []),
                    inc=("cDVE" if h == FT - 1 else None))
            st["merge_done"] = s.c("cDVE")
            ln_stats(c, which=1)
            ln_apply1()
            st["ln1_done"] = s.c("cDVE")

        def ln_stats(c, which):
            wsrc = s.c("cDVE")
            wdpw = p1_pw if (c == 0 and which == 1) else 0
            sq_free = [0, 0]
            for h in range(FT):
                par = h % 2
                waits = []
                if h == 0:
                    waits += [("cDVE", wsrc)]
                    if wdpw:
                        waits += [("dPWA", wdpw)]
                if sq_free[par]:
                    waits += [("cPE", sq_free[par])]
                s.emit("vector", lambda v, h=h, par=par: v.tensor_mul(
                    MIDQ[:, 512 * par:512 * par + 512], res_t(h), res_t(h)),
                    waits=waits, inc="cDVE")
                wsq = s.c("cDVE")
                s.emit("tensor", lambda t, h=h: t.matmul(
                    PS[2][0:1, 0:T], CONESB[:, :], res_t(h),
                    start=(h == 0), stop=(h == FT - 1)),
                    waits=[("cDVE", wsq)] + ([("cACT", st["res_ready"])] if h == 0 else []))
                s.emit("tensor", lambda t, h=h, par=par: t.matmul(
                    PS[3][0:1, 0:T], CONESB[:, :], MIDQ[:, 512 * par:512 * par + 512],
                    start=(h == 0), stop=(h == FT - 1)),
                    inc="cPE")
                sq_free[par] = s.c("cPE")
            wstat = s.c("cPE")
            steps = [
                lambda v: v.tensor_scalar_mul(SVEC[0:1, 0:512], PS[2][0:1, 0:T], 1.0 / D),
                lambda v: v.tensor_scalar_mul(SVEC[0:1, 512:1024], PS[3][0:1, 0:T], 1.0 / D),
                lambda v: v.tensor_mul(SVEC[0:1, 1024:1536], SVEC[0:1, 0:512], SVEC[0:1, 0:512]),
                lambda v: v.tensor_sub(SVEC[0:1, 1536:2048], SVEC[0:1, 512:1024], SVEC[0:1, 1024:1536]),
                lambda v: v.tensor_scalar_add(SVEC[0:1, 2048:2560], SVEC[0:1, 1536:2048], EPS),
                lambda v: v.reciprocal(SVEC[0:1, 2560:3072], SVEC[0:1, 2048:2560]),
            ]
            wprev = wstat
            for i, fstep in enumerate(steps):
                s.emit("vector", fstep,
                       waits=([("cPE", wstat)] if i == 0 else [("cDVE", wprev)]),
                       inc="cDVE")
                wprev = s.c("cDVE")
            s.emit("scalar", lambda sc: sc.activation(
                out=SVEC[0:1, 1024:1536], in_=SVEC[0:1, 2560:3072], func=AF.Sqrt),
                waits=[("cDVE", wprev)], inc="cACT")
            wrstd = s.c("cACT")
            s.emit("tensor", lambda t: t.matmul(
                PS[2][:, 0:T], CONESF[:, :], SVEC[0:1, 0:512], start=True, stop=True),
                waits=[("cACT", wrstd)])
            s.emit("tensor", lambda t: t.matmul(
                PS[3][:, 0:T], CONESF[:, :], SVEC[0:1, 1024:1536], start=True, stop=True),
                inc="cPE")
            return s.c("cPE")

        def ln_apply1():
            wbc = s.c("cPE")
            for h in range(FT):
                par = h % 2
                s.emit("vector", lambda v, h=h, par=par: v.tensor_sub(
                    GSIL[:, 512 * par:512 * par + 512], res_t(h), PS[2][:, 0:T]),
                    waits=([("cPE", wbc), ("dST", st["store_done"])]
                           if h == 0 else []), inc="cDVE")
                s.emit("vector", lambda v, h=h, par=par: v.tensor_mul(
                    res_t(h), GSIL[:, 512 * par:512 * par + 512], PS[3][:, 0:T]),
                    inc="cDVE")

        def phase_q_gu(c, q):
            n, s0 = QN[q], QS0[q]
            wld = {"g": 0, "u": 0}
            for blk0 in range(0, n, 4):
                bn = min(4, n - blk0)
                sl0 = s0 + blk0
                for mat, mTb, buf in (("g", wgT_b, WBUF), ("u", wuT_b, QKVO)):
                    need = gT_done[(mat, q)]
                    wfsem = "dWFg" if mat == "g" else "dWFu"
                    if mat == "g":
                        wfree_pe = st["wbuf_free"]
                        extra = []
                    else:
                        wfree_pe = st["qkvo_free_pe"]
                        extra = [("cDVE", st["ln1_done"]), ("dST", st["store_done"])]
                    s.emit("sync", lambda sy, mTb=mTb, buf=buf, sl0=sl0, bn=bn: sy.dma_start(
                        out=buf[:, 0:512 * FT].rearrange("p (hh j) -> p hh j", j=512)[:, :, 0:128 * bn],
                        in_=mTb[:, P * sl0:P * sl0 + P * bn]
                        .rearrange("(hh p) j -> p hh j", p=P)),
                        waits=[("dPWCgu", need), ("cPE", wfree_pe)] + extra,
                        inc=wfsem, amt=16, dma=True)
                    wld[mat] = s.c(wfsem)
                for bi in range(bn):
                    sl = sl0 + bi
                    par = sl % 2
                    for h in range(FT):
                        waits = [("dWFg", wld["g"])]
                        if h == 0:
                            waits += [("cDVE", max(st["ln1_done"], st["dn_free"])),
                                      ("cACT", st["sil_free"][par])]
                        s.emit("tensor", lambda t, h=h, bi=bi, par=par: t.matmul(
                            PS[par][:, 0:T],
                            WBUF[:, 512 * h + 128 * bi:512 * h + 128 * bi + 128],
                            res_t(h), start=(h == 0), stop=(h == FT - 1)),
                            waits=waits, inc=("cPE" if h == FT - 1 else None))
                    wgmm = s.c("cPE")
                    s.emit("scalar", lambda sc, par=par: sc.activation(
                        out=GSIL[:, 512 * par:512 * par + 512], in_=PS[par][:, 0:T],
                        func=AF.Silu, scale=CSCL[:, 0:1]),
                        waits=[("cPE", wgmm), ("cDVE", st["mid_free"][par])], inc="cACT")
                    st["sil_free"][par] = s.c("cACT")
                    wsil = s.c("cACT")
                    for h in range(FT):
                        waits = [("dWFu", wld["u"])]
                        if h == 0:
                            waits += [("cACT", wsil),
                                      ("cDVE", max(st["mid_free"][par], st["dn_free"]))]
                        s.emit("tensor", lambda t, h=h, bi=bi, par=par: t.matmul(
                            PS[2 + par][:, 0:T],
                            QKVO[:, 512 * h + 128 * bi:512 * h + 128 * bi + 128],
                            res_t(h), start=(h == 0), stop=(h == FT - 1)),
                            waits=waits, inc=("cPE" if h == FT - 1 else None))
                    wumm = s.c("cPE")
                    mq = sl - s0
                    s.emit("vector", lambda v, mq=mq, par=par: v.scalar_tensor_tensor(
                        out=MIDQ[:, 512 * mq:512 * mq + 512], in0=PS[2 + par][:, 0:T],
                        scalar=CSCL[:, 1:2], in1=GSIL[:, 512 * par:512 * par + 512],
                        op0=ALU.mult, op1=ALU.mult),
                        waits=[("cPE", wumm)], inc="cDVE")
                    st["mid_free"][par] = s.c("cDVE")
                st["wbuf_free"] = s.c("cPE")
                st["qkvo_free_pe"] = s.c("cPE")
            return s.c("cDVE")

        def phase_q_down(c, q, wmid):
            n, s0 = QN[q], QS0[q]
            for fg in range(8):
                sub = []
                for sb0 in range(0, n, 8):
                    sub.append((sb0, min(8, n - sb0)))
                for (sb0, sbn) in sub:
                    par8 = (sb0 // 8) % 2
                    slot = WDN[:, 4096 * par8:4096 * par8 + 512 * sbn]
                    need = dT_done[s0 + sb0 + sbn - 1]
                    wds = "dWD0" if par8 == 0 else "dWD1"
                    s.emit("sync", lambda sy, s0=s0, sb0=sb0, sbn=sbn, fg=fg, sl_=slot:
                           sy.dma_start(
                               out=sl_.rearrange("p (ss j) -> p ss j", j=512),
                               in_=wdT_b[P * (s0 + sb0):P * (s0 + sb0) + P * sbn,
                                         512 * fg:512 * fg + 512]
                               .rearrange("(ss p) j -> p ss j", p=P)),
                           waits=[("dPWCwd", need), ("cPE", st["wdn_free"][par8]),
                                  ("dST", st["store_done"])],
                           inc=wds, amt=16, dma=True)
                    wldd = s.c(wds)
                    for si in range(sbn):
                        sl = s0 + sb0 + si
                        first = (sb0 + si == 0)
                        last = (sb0 + si == n - 1)
                        for tf in range(4):
                            waits = [(wds, wldd)]
                            if first and tf == 0:
                                waits += [("cDVE", max(wmid, st["dn_free"]))]
                                if c == 0 and q == 0 and fg == 0:
                                    waits += [("dPWCgu", p3_pw_gu),
                                              ("dPWCwd", p3_pw_wd)]
                            s.emit("tensor", lambda t, si=si, tf=tf, sl=sl, par8=par8,
                                   s0=s0, first=first, last=last: t.matmul(
                                       PS[tf][:, 0:T],
                                       WDN[:, 4096 * par8 + 512 * si + 128 * tf:
                                           4096 * par8 + 512 * si + 128 * tf + 128],
                                       MIDQ[:, 512 * (sl - s0):512 * (sl - s0) + 512],
                                       start=first, stop=last),
                                   waits=waits,
                                   inc=("cPE" if tf == 3 and si == sbn - 1 else None))
                    st["wdn_free"][par8] = s.c("cPE")
                wdn = s.c("cPE")
                for tf in range(4):
                    h = 4 * fg + tf
                    pslice = PART[:, T * h:T * h + T]
                    if q == 0:
                        s.emit("vector", lambda v, tf=tf, ps=pslice: v.tensor_scalar(
                            out=ps, in0=PS[tf][:, 0:T], scalar1=CSCL[:, 2:3],
                            scalar2=None, op0=ALU.mult),
                            waits=([("cPE", wdn)] if tf == 0 else []),
                            inc=("cDVE" if tf == 3 else None))
                    elif q < 3:
                        s.emit("vector", lambda v, tf=tf, ps=pslice: v.scalar_tensor_tensor(
                            out=ps, in0=PS[tf][:, 0:T], scalar=CSCL[:, 2:3], in1=ps,
                            op0=ALU.mult, op1=ALU.add),
                            waits=([("cPE", wdn)] if tf == 0 else []),
                            inc=("cDVE" if tf == 3 else None))
                    else:
                        s.emit("vector", lambda v, tf=tf, ps=pslice: v.scalar_tensor_tensor(
                            out=GSIL[:, 0:512], in0=PS[tf][:, 0:T], scalar=CSCL[:, 2:3],
                            in1=ps, op0=ALU.mult, op1=ALU.add),
                            waits=([("cPE", wdn), ("dST", st["store_done"])]
                                   if tf == 0 else []), inc="cDVE")
                        wtmp = s.c("cDVE")
                        s.emit("vector", lambda v, h=h: v.tensor_add(
                            res_t(h), res_t(h), GSIL[:, 0:512]),
                            waits=[("cDVE", wtmp)], inc="cDVE")
                st["dn_free"] = s.c("cDVE")

        def phase_out(c):
            wbc = ln_stats(c, which=2)
            ev_free = [0, 0]
            prev_store = 0
            for half in range(2):
                for h in range(FT):
                    waits = []
                    if h == 0:
                        waits += [("cPE", wbc)]
                        if half == 1:
                            waits += [("cPE", st["out_tr"])]
                    s.emit("vector", lambda v, h=h, half=half: v.tensor_sub(
                        GSIL[:, 512:768], res_t(h)[:, 256 * half:256 * half + 256],
                        PS[2][:, 256 * half:256 * half + 256]),
                        waits=waits, inc="cDVE")
                    s.emit("vector", lambda v, h=h, half=half: v.tensor_mul(
                        h2f[:, 256 * h:256 * h + 256], GSIL[:, 512:768],
                        PS[3][:, 256 * half:256 * half + 256]),
                        inc="cDVE")
                wapp = s.c("cDVE")
                for tt in (2 * half, 2 * half + 1):
                    for h in range(FT):
                        bank = PS[h % 2]
                        waits = [("cACT", ev_free[h % 2])] if ev_free[h % 2] else []
                        if h == 0:
                            waits += [("cDVE", wapp)]
                            if prev_store:
                                waits += [("dST", prev_store)]
                        s.emit("tensor", lambda t, h=h, tt=tt, b=bank: t.transpose(
                            b[:, 0:P],
                            h2f[:, 256 * h + 128 * (tt % 2):256 * h + 128 * (tt % 2) + 128],
                            CEYEF[:, :]),
                            waits=waits, inc="cPE")
                        wtr = s.c("cPE")
                        s.emit("scalar", lambda sc, h=h, b=bank: sc.copy(
                            out=ostf[:, h:4096:32], in_=b[:, 0:P]),
                            waits=[("cPE", wtr)], inc="cACT")
                        ev_free[h % 2] = s.c("cACT")
                    wev = s.c("cACT")
                    s.emit("sync", lambda sy, c=c, tt=tt: sy.dma_start(
                        out=y_out[T * c + P * tt:T * c + P * tt + P, :], in_=ostf[:, :]),
                        waits=[("cACT", wev)], inc="dST", amt=16, dma=True)
                    prev_store = s.c("dST")
                st["out_tr"] = s.c("cPE")
            st["store_done"] = s.c("dST")

        def dbg_dump(name, src_ap, waits):
            if dbg:
                s.emit("sync", lambda sy, d=dbg_t[name], sp=src_ap: sy.dma_start(
                    out=d[:, 0:sp.shape[1]], in_=sp), waits=waits,
                    inc="dST", amt=16, dma=True)
                st["store_done"] = s.c("dST")

        for c in range(nchunk):
            if upto < 1:
                break
            phase_A(c)
            if c == 0:
                dbg_dump("d_x", RES[:, :], [("cACT", st["res_ready"])])
            if upto < 2:
                break
            phase_B(c)
            if c == 0:
                dbg_dump("d_qkv", QKVO[:, 0:12288], [("cDVE", st["qkv_evac"])])
            if upto < 3:
                break
            phase_C(c)
            if c == 0:
                dbg_dump("d_ovn", QKVO[:, 12288:16384],
                         [("cDVE", max(mid["norm"][2], mid["norm"][3]))])
            if upto < 4:
                break
            phase_D(c)
            if c == 0:
                dbg_dump("d_h1", RES[:, :], [("cDVE", st["ln1_done"])])
            if upto < 5:
                break
            for q in range(4):
                wmid = phase_q_gu(c, q)
                if upto < 6:
                    break
                phase_q_down(c, q, wmid)
            if c == 0:
                dbg_dump("d_h2", RES[:, :], [("cDVE", st["dn_free"])])
            if upto < 7:
                break
            phase_out(c)

        final_st = s.c("dST")

        def replay(eng_obj, name):
            for ws, fn, inc, amt in s.ops[name]:
                for sm, v in ws:
                    eng_obj.wait_ge(sems[sm], v)
                ins = fn(eng_obj)
                if inc is not None:
                    ins.then_inc(sems[inc], amt)
            if name == "sync":
                eng_obj.wait_ge(sems["dST"], final_st)

        @block.sync
        def _(sync):
            replay(sync, "sync")

        @block.scalar
        def _(scalar):
            replay(scalar, "scalar")

        @block.vector
        def _(vector):
            replay(vector, "vector")

        @block.tensor
        def _(tensor):
            replay(tensor, "tensor")

        @block.gpsimd
        def _(gpsimd):
            replay(gpsimd, "gpsimd")

    return nc


def kernel(L0_attn_x, L0_attn_Wq, L0_attn_bq, L0_attn_Wk, L0_attn_bk,
           L0_attn_Wv, L0_attn_bv, L0_ffn_w_gate, L0_ffn_w_up, L0_ffn_w_down,
           L0_ffn_scale_gate, L0_ffn_scale_up, L0_ffn_scale_down):
    import ml_dtypes
    x = np.ascontiguousarray(np.asarray(L0_attn_x, dtype=np.float32).reshape(16384, D))
    wq_h = np.ascontiguousarray(np.asarray(L0_attn_Wq, np.float32).reshape(D, 4096))
    wk_h = np.ascontiguousarray(np.asarray(L0_attn_Wk, np.float32).reshape(D, 1024))
    wv_h = np.ascontiguousarray(np.asarray(L0_attn_Wv, np.float32).reshape(D, 1024))
    bq_h = np.ascontiguousarray(np.asarray(L0_attn_bq, np.float32))
    bk_h = np.ascontiguousarray(np.asarray(L0_attn_bk, np.float32))
    bv_h = np.ascontiguousarray(np.asarray(L0_attn_bv, np.float32))
    wg_h = np.ascontiguousarray(np.asarray(L0_ffn_w_gate, np.int32))
    wu_h = np.ascontiguousarray(np.asarray(L0_ffn_w_up, np.int32))
    wd_h = np.ascontiguousarray(np.asarray(L0_ffn_w_down, np.int32))
    scl_h = np.ascontiguousarray(np.broadcast_to(
        np.stack([np.asarray(L0_ffn_scale_gate, np.float32).reshape(()),
                  np.asarray(L0_ffn_scale_up, np.float32).reshape(()),
                  np.asarray(L0_ffn_scale_down, np.float32).reshape(())]).reshape(1, 3),
        (P, 3)))
    eyef_h = np.eye(P, dtype=np.float32)
    eyeb_h = np.eye(P).astype(ml_dtypes.bfloat16)
    onesb_h = np.ones((P, 1)).astype(ml_dtypes.bfloat16)
    onesf_h = np.ones((1, P), dtype=np.float32)

    nc = build_nc()
    in_maps = []
    for cid in range(NCORE):
        in_maps.append({
            "x_in": np.ascontiguousarray(x[TOK_CORE * cid: TOK_CORE * (cid + 1)]),
            "wq": wq_h, "wk": wk_h, "wv": wv_h,
            "bq": bq_h, "bk": bk_h, "bv": bv_h,
            "wg": wg_h, "wu": wu_h, "wd": wd_h,
            "scl": scl_h, "eyef": eyef_h, "eyeb": eyeb_h,
            "onesb": onesb_h, "onesf": onesf_h,
        })
    res = run_bass_kernel_spmd(nc, in_maps, core_ids=list(range(NCORE)))
    out = np.concatenate([r["y_out"] for r in res.results], axis=0)
    return out.reshape(2, 8192, D)


# version 02
# speedup vs baseline: 1.0819x; 1.0819x over previous
import numpy as np
from contextlib import ExitStack

import concourse.bass as bass
import concourse.mybir as mybir
from concourse.bass_utils import run_bass_kernel_spmd

F32 = mybir.dt.float32
BF16 = mybir.dt.bfloat16
I32 = mybir.dt.int32
ALU = mybir.AluOpType
AF = mybir.ActivationFunctionType
AX = mybir.AxisListType.X

P = 128
D = 4096
HID = 11008
NSLAB = 86
FT = 32
T = 512
NCHUNK = 4
TOK_CORE = T * NCHUNK
NCORE = 8
WIN = 64
RSQ = float(1.0 / np.sqrt(128.0))
EPS = 1e-5
QN = [22, 22, 21, 21]
QS0 = [0, 22, 44, 65]
OVN0 = 12288


class Sched:
    ENGINES = ("sync", "scalar", "vector", "tensor", "gpsimd")

    def __init__(self):
        self.ops = {e: [] for e in self.ENGINES}
        self.cnt = {}

    ENG_SEM = {"vector": "cDVE", "scalar": "cACT", "gpsimd": "cPOOL"}

    def emit(self, eng, fn, waits=(), inc=None, amt=1, dma=False):
        ws = []
        if not dma and eng in self.ENG_SEM:
            osem = self.ENG_SEM[eng]
            prev = self.cnt.get(osem, 0)
            waits = list(waits) + [(osem, prev)]
            assert inc is None or inc == osem, f"{eng} compute op inc={inc}"
            inc, amt = osem, 1
        for sm, v in waits:
            if v > 0:
                assert v <= self.cnt.get(sm, 0), \
                    f"wait on {sm}>={v} but only {self.cnt.get(sm, 0)} emitted"
                ws.append((sm, v))
        self.ops[eng].append((ws, fn, inc, amt))
        if inc is not None:
            self.cnt[inc] = self.cnt.get(inc, 0) + amt
        return self.cnt.get(inc, 0) if inc else None

    def c(self, sem):
        return self.cnt.get(sem, 0)


def build_nc(nchunk=NCHUNK, dbg=False, upto=99, cupto=99):
    nc = bass.Bass("TRN2", target_bir_lowering=False)
    tok_core = T * nchunk

    x_in = nc.dram_tensor("x_in", [tok_core, D], F32, kind="ExternalInput").ap()
    wq = nc.dram_tensor("wq", [D, 4096], F32, kind="ExternalInput").ap()
    wk = nc.dram_tensor("wk", [D, 1024], F32, kind="ExternalInput").ap()
    wv = nc.dram_tensor("wv", [D, 1024], F32, kind="ExternalInput").ap()
    bq = nc.dram_tensor("bq", [P, 32], F32, kind="ExternalInput").ap()
    bk = nc.dram_tensor("bk", [P, 8], F32, kind="ExternalInput").ap()
    bv = nc.dram_tensor("bv", [P, 8], F32, kind="ExternalInput").ap()
    wg = nc.dram_tensor("wg", [HID, D], I32, kind="ExternalInput").ap()
    wu = nc.dram_tensor("wu", [HID, D], I32, kind="ExternalInput").ap()
    wd = nc.dram_tensor("wd", [D, HID], I32, kind="ExternalInput").ap()
    scl = nc.dram_tensor("scl", [P, 3], F32, kind="ExternalInput").ap()
    eyef = nc.dram_tensor("eyef", [P, P], F32, kind="ExternalInput").ap()
    eyeb = nc.dram_tensor("eyeb", [P, P], BF16, kind="ExternalInput").ap()
    onesb = nc.dram_tensor("onesb", [P, 1], BF16, kind="ExternalInput").ap()
    onesf = nc.dram_tensor("onesf", [1, P], F32, kind="ExternalInput").ap()
    y_out = nc.dram_tensor("y_out", [tok_core, D], F32, kind="ExternalOutput").ap()

    wattn_b = nc.dram_tensor("wattn_b", [D, 3072], BF16, kind="Internal").ap()
    wg_b = nc.dram_tensor("wg_b", [HID, D], BF16, kind="Internal").ap()
    wu_b = nc.dram_tensor("wu_b", [HID, D], BF16, kind="Internal").ap()
    wd_b = nc.dram_tensor("wd_b", [D, HID], BF16, kind="Internal").ap()
    wgT_b = nc.dram_tensor("wgT_b", [D, HID], BF16, kind="Internal").ap()
    wuT_b = nc.dram_tensor("wuT_b", [D, HID], BF16, kind="Internal").ap()
    wdT_b = nc.dram_tensor("wdT_b", [HID, D], BF16, kind="Internal").ap()

    dbg_t = {}
    if dbg:
        for name in ("d_x", "d_qkv", "d_ovn", "d_h1", "d_h2"):
            dbg_t[name] = nc.dram_tensor(name, [P, 16384], BF16, kind="ExternalOutput").ap()
        for name in ("d_bqkt", "d_ebuf", "d_bet", "d_bpack"):
            dbg_t[name] = nc.dram_tensor(name, [P, 4672], BF16, kind="ExternalOutput").ap()
        dbg_t["d_pov"] = nc.dram_tensor("d_pov", [P, 512], F32, kind="ExternalOutput").ap()

    s = Sched()

    with ExitStack() as ctx:
        en = ctx.enter_context
        RES = en(nc.sbuf_tensor("RES", [P, FT * T], BF16))
        QKVO = en(nc.sbuf_tensor("QKVO", [P, 16384], BF16))
        WBUF = en(nc.sbuf_tensor("WBUF", [P, 16384], BF16))
        MIDQ = en(nc.sbuf_tensor("MIDQ", [P, 11264], BF16))
        PART = en(nc.sbuf_tensor("PART", [P, 16384], BF16))
        BPACK = en(nc.sbuf_tensor("BPACK", [P, 1600], BF16))
        OSTG = en(nc.sbuf_tensor("OSTG", [P, 8192], BF16))
        WATT = en(nc.sbuf_tensor("WATT", [P, 1024], BF16))
        EBUF = en(nc.sbuf_tensor("EBUF", [P, 1024], BF16))
        BET = en(nc.sbuf_tensor("BET", [P, 1024], BF16))
        BQKT = en(nc.sbuf_tensor("BQKT", [P, 2048], BF16))
        GSIL = en(nc.sbuf_tensor("GSIL", [P, 1024], F32))
        NM = en(nc.sbuf_tensor("NM", [P, 2], F32))
        RW = en(nc.sbuf_tensor("RW", [P, 64], F32))
        SVEC = en(nc.sbuf_tensor("SVEC", [1, 3072], F32))
        CEYEF = en(nc.sbuf_tensor("CEYEF", [P, P], F32))
        CEYEB = en(nc.sbuf_tensor("CEYEB", [P, P], BF16))
        CONESB = en(nc.sbuf_tensor("CONESB", [P, 1], BF16))
        CONESF = en(nc.sbuf_tensor("CONESF", [1, P], F32))
        CBQ = en(nc.sbuf_tensor("CBQ", [P, 48], F32))
        CBK = en(nc.sbuf_tensor("CBK", [P, 8], F32))
        CBV = en(nc.sbuf_tensor("CBV", [P, 8], F32))
        CSCL = en(nc.sbuf_tensor("CSCL", [P, 3], F32))
        CBQS = en(nc.sbuf_tensor("CBQS", [P, 8], F32))

        PS = [en(nc.psum_tensor(f"PS{i}", [P, 512], F32)) for i in range(4)]
        PTB = [en(nc.psum_tensor(f"PTB{i}", [P, 1024], BF16)) for i in range(4)]

        sem_names = ["dX0", "dX1", "dWA", "dWA0", "dWA1", "dWFg0", "dWFg1", "dWFu0", "dWFu1",
                     "dWFgu", "dWFwd", "dWD0", "dWD1", "dCast", "dRawG",
                     "dRawU", "dPWA", "dPWBg", "dPWBu", "dPWCgu", "dPWCwd",
                     "dST", "dMisc", "cPE", "cACT", "cDVE", "cPOOL"]
        sems = {n: en(nc.semaphore(n)) for n in sem_names}
        block = en(nc.Block())

        xstage = WBUF[:, :].bitcast(F32)
        partf = PART[:, :].bitcast(F32)
        h2f = PART[:, :].bitcast(F32)
        ostf = OSTG[:, :].bitcast(F32)
        WDN = OSTG

        def res_t(h):
            return RES[:, T * h: T * h + T]

        for dst, src in [(CEYEF[:, :], eyef), (CEYEB[:, :], eyeb),
                         (CONESB[:, :], onesb), (CONESF[:, :], onesf),
                         (CBQ[:, 0:32], bq), (CBK[:, :], bk), (CBV[:, :], bv),
                         (CSCL[:, :], scl)]:
            s.emit("sync", lambda sy, d=dst, sr=src: sy.dma_start(out=d, in_=sr),
                   inc="dMisc", amt=16, dma=True)
        w_misc = s.c("dMisc")

        s.emit("vector", lambda v: v.tensor_add(
            CBQ[:, 32:48], CBQ[:, 0:32:2], CBQ[:, 1:32:2]),
            waits=[("dMisc", w_misc)])
        s.emit("vector", lambda v: v.tensor_add(
            CBQ[:, 0:8], CBQ[:, 32:48:2], CBQ[:, 33:48:2]))
        s.emit("vector", lambda v: v.tensor_scalar_mul(CBQS[:, :], CBQ[:, 0:8], RSQ))
        s.emit("vector", lambda v: v.memset(BPACK[:, 0:1600], 0.0))
        s.emit("vector", lambda v: v.memset(
            BPACK[:, 1024:1600].rearrange("p (j c) -> p j c", c=9)[:, :, 8], 1.0))

        wq_src = wq.rearrange("(d h) c -> h d c", h=32)
        wk_src = wk.rearrange("(d h) c -> h d c", h=32)
        wv_src = wv.rearrange("(d h) c -> h d c", h=32)
        for h in range(FT):
            wprev = s.c("cDVE")
            s.emit("sync", lambda sy, h=h: sy.dma_start(out=partf[:, 0:4096], in_=wq_src[h, :, :]),
                   waits=[("cDVE", wprev)], inc="dWA", amt=16, dma=True)
            s.emit("sync", lambda sy, h=h: sy.dma_start(out=partf[:, 4096:5120], in_=wk_src[h, :, :]),
                   inc="dWA", amt=16, dma=True)
            s.emit("sync", lambda sy, h=h: sy.dma_start(out=partf[:, 5120:6144], in_=wv_src[h, :, :]),
                   inc="dWA", amt=16, dma=True)
            wload = s.c("dWA")
            wpw = s.c("dPWA")

            s.emit("vector", lambda v: v.tensor_add(
                partf[:, 6144:8192], partf[:, 0:4096:2], partf[:, 1:4096:2]),
                waits=[("dWA", wload), ("dPWA", wpw)])
            s.emit("vector", lambda v: v.tensor_add(
                partf[:, 0:1024], partf[:, 6144:8192:2], partf[:, 6145:8192:2]))
            s.emit("vector", lambda v: v.tensor_scalar(
                out=MIDQ[:, 0:1024].rearrange("p (g d) -> p g d", d=P),
                in0=partf[:, 0:1024].rearrange("p (d g) -> p g d", g=8),
                scalar1=RSQ, scalar2=None, op0=ALU.mult))
            s.emit("vector", lambda v: v.tensor_copy(
                out=MIDQ[:, 1024:2048].rearrange("p (g e) -> p g e", e=P),
                in_=partf[:, 4096:5120].rearrange("p (e g) -> p g e", g=8)))
            s.emit("vector", lambda v: v.tensor_copy(
                out=MIDQ[:, 2048:3072].rearrange("p (g e) -> p g e", e=P),
                in_=partf[:, 5120:6144].rearrange("p (e g) -> p g e", g=8)))
            wprep = s.c("cDVE")
            s.emit("scalar", lambda sc, h=h: sc.dma_start(
                out=wattn_b[P * h: P * h + P, :], in_=MIDQ[:, 0:3072]),
                waits=[("cDVE", wprep)], inc="dPWA", amt=16, dma=True)
        p1_pw = s.c("dPWA")
        p1_dve = s.c("cDVE")

        wd_src = wd.rearrange("(d h) c -> h d c", h=32)
        wd_dst = wd_b.rearrange("(h d) c -> h d c", d=P)
        SEG = 5632
        cast_done = {}
        s.emit("gpsimd", lambda g: g.dma_start(out=wg_b[0:SEG, :], in_=wg[0:SEG, :]),
               waits=[("dCast", 0)], inc="dCast", amt=16, dma=True)
        cast_done[("g", 0)] = cast_done[("g", 1)] = s.c("dCast")
        s.emit("gpsimd", lambda g: g.dma_start(out=wu_b[0:SEG, :], in_=wu[0:SEG, :]),
               waits=[("dCast", 16)], inc="dCast", amt=16, dma=True)
        cast_done[("u", 0)] = cast_done[("u", 1)] = s.c("dCast")
        s.emit("gpsimd", lambda g: g.dma_start(out=wd_dst[:, :, 0:SEG], in_=wd_src[:, :, 0:SEG]),
               waits=[("dCast", 32)], inc="dCast", amt=16, dma=True)
        cast_done[("d", 0)] = cast_done[("d", 1)] = s.c("dCast")
        s.emit("gpsimd", lambda g: g.dma_start(out=wd_dst[:, :, SEG:HID], in_=wd_src[:, :, SEG:HID]),
               waits=[("dCast", 48)], inc="dCast", amt=16, dma=True)
        cast_done[("d", 2)] = cast_done[("d", 3)] = s.c("dCast")
        s.emit("gpsimd", lambda g: g.dma_start(out=wg_b[SEG:HID, :], in_=wg[SEG:HID, :]),
               waits=[("dCast", 64)], inc="dCast", amt=16, dma=True)
        cast_done[("g", 2)] = cast_done[("g", 3)] = s.c("dCast")
        s.emit("gpsimd", lambda g: g.dma_start(out=wu_b[SEG:HID, :], in_=wu[SEG:HID, :]),
               waits=[("dCast", 80)], inc="dCast", amt=16, dma=True)
        cast_done[("u", 2)] = cast_done[("u", 3)] = s.c("dCast")

        QHB = [QN[i] * P for i in range(4)]
        QH0 = [QS0[i] * P for i in range(4)]
        gT_done = {}
        dT_done = {}

        def emit_P3():
            for q in range(4):
                for mat, mb, mTb in (("g", wg_b, wgT_b), ("u", wu_b, wuT_b)):
                    need = cast_done[(mat, q)]
                    for b in range(FT):
                        wpw = s.c("dPWCgu")
                        waits = [("dCast", need), ("dPWCgu", wpw)]
                        if q == 0 and b == 0 and mat == "g":
                            waits.append(("cDVE", p1_dve))
                        s.emit("sync", lambda sy, mb=mb, q=q, b=b: sy.dma_start(
                            out=PART[:, 0:QHB[q]],
                            in_=mb[QH0[q]:QH0[q] + QHB[q], P * b:P * b + P],
                            transpose=True),
                            waits=waits, inc="dWFgu", amt=16, dma=True)
                        wrd = s.c("dWFgu")
                        s.emit("scalar", lambda sc, mTb=mTb, q=q, b=b: sc.dma_start(
                            out=mTb.rearrange("(h x) c -> x h c", x=P)
                            [4 * b:4 * b + 4, :, QH0[q]:QH0[q] + QHB[q]],
                            in_=PART[:, 0:QHB[q]]),
                            waits=[("dWFgu", wrd)], inc="dPWCgu", amt=16, dma=True)
                    gT_done[(mat, q)] = s.c("dPWCgu")
                for sl in range(QS0[q], QS0[q] + QN[q]):
                    wpw = s.c("dPWCwd")
                    waits = [("dCast", cast_done[("d", q)]), ("dPWCwd", wpw)]
                    if sl == 0:
                        waits.append(("cDVE", p1_dve))
                    s.emit("sync", lambda sy, sl=sl: sy.dma_start(
                        out=PART[:, 8192:12288],
                        in_=wd_b[0:D, P * sl:P * sl + P], transpose=True),
                        waits=waits, inc="dWFwd", amt=16, dma=True)
                    wrd = s.c("dWFwd")
                    s.emit("scalar", lambda sc, sl=sl: sc.dma_start(
                        out=wdT_b[P * sl:P * sl + P, :], in_=PART[:, 8192:12288]),
                        waits=[("dWFwd", wrd)], inc="dPWCwd", amt=16, dma=True)
                    dT_done[sl] = s.c("dPWCwd")
            return s.c("dPWCgu"), s.c("dPWCwd")

        st = dict(
            store_done=0,
            xslot_free=[0, 0],
            res_ready=0,
            qkv_evac=0,
            wbuf_free=0,
            qkvo_free_pe=0,
            merge_done=0, ln1_done=0,
            sil_free=[0, 0],
            mid_free=[0, 0],
            wdn_free=[0, 0],
            dn_free=0,
            out_tr=0,
        )
        mid = dict(mm1=[0, 0], exp=[0, 0], nmx=[0, 0], etr=[0, 0], etev=[0, 0],
                   mm3=[0, 0], bqkev=[0, 0], btr=[0, 0], norm={2: 0, 3: 0}, bld=0)

        def phase_A(c):
            for tt in range(4):
                par = tt % 2
                slot = xstage[:, 4096 * par: 4096 * par + 4096]
                dxs = "dX0" if par == 0 else "dX1"
                s.emit("sync", lambda sy, c=c, tt=tt, sl=slot: sy.dma_start(
                    out=sl, in_=x_in[T * c + P * tt: T * c + P * (tt + 1), :]),
                    waits=[("cPE", st["xslot_free"][par]), ("dST", st["store_done"])],
                    inc=dxs, amt=16, dma=True)
                wx = s.c(dxs)
                for h in range(FT):
                    bank = PS[2 + (h % 2)]
                    wev = s.c("cACT")
                    s.emit("tensor", lambda t, sl=slot, h=h, b=bank, dxs=dxs: t.transpose(
                        b[:, 0:P], sl[:, h:4096:32], CEYEF[:, :]),
                        waits=[(dxs, wx), ("cACT", wev - 1)], inc="cPE")
                    wtr = s.c("cPE")
                    s.emit("scalar", lambda sc, h=h, tt=tt, b=bank: sc.copy(
                        out=res_t(h)[:, P * tt:P * (tt + 1)], in_=b[:, 0:P]),
                        waits=[("cPE", wtr)], inc="cACT")
                st["xslot_free"][par] = s.c("cPE")
            st["res_ready"] = s.c("cACT")
            st["wbuf_free"] = s.c("cPE")

        def phase_B(c):
            bank_free_dve = s.c("cDVE")
            for grp in range(6):
                grp_mm_base = s.c("cPE")
                for h in range(FT):
                    par = h % 2
                    ws = WATT[:, 512 * par: 512 * par + 512]
                    wfree = grp_mm_base + h - 1 if h >= 2 else grp_mm_base
                    was = "dWA0" if par == 0 else "dWA1"
                    s.emit("sync", lambda sy, grp=grp, h=h, ws=ws: sy.dma_start(
                        out=ws, in_=wattn_b[P * h: P * h + P, 512 * grp: 512 * grp + 512]),
                        waits=[("cPE", wfree), ("dPWA", p1_pw)], inc=was, amt=16, dma=True)
                    wld = s.c(was)
                    for m6 in range(4):
                        waits = [(was, wld)]
                        if h == 0:
                            waits += [("cACT", st["res_ready"]), ("cDVE", bank_free_dve)]
                        s.emit("tensor", lambda t, m6=m6, h=h, ws=ws: t.matmul(
                            PS[m6][:, 0:T], ws[:, 128 * m6:128 * m6 + 128], res_t(h),
                            start=(h == 0), stop=(h == FT - 1)),
                            waits=waits, inc=("cPE" if m6 == 3 else None))
                wmm = s.c("cPE")
                for m6 in range(4):
                    m = grp * 4 + m6
                    if m < 8:
                        bias = CBQS[:, m:m + 1]
                    elif m < 16:
                        bias = CBK[:, m - 8:m - 7]
                    else:
                        bias = CBV[:, m - 16:m - 15]
                    s.emit("vector", lambda v, m=m, m6=m6, bias=bias: v.tensor_scalar(
                        out=QKVO[:, T * m: T * m + T], in0=PS[m6][:, 0:T],
                        scalar1=bias, scalar2=None, op0=ALU.add),
                        waits=[("cPE", wmm)], inc="cDVE")
                bank_free_dve = s.c("cDVE")
            st["qkv_evac"] = s.c("cDVE")

        def phase_C(c):
            for w in range(T // WIN):
                w0 = WIN * w
                bw = [("cDVE", st["qkv_evac"]),
                      ("cPE", max(mid["btr"][0], mid["btr"][1],
                                  mid["mm3"][0], mid["mm3"][1])),
                      ("dST", st["store_done"])]
                for g in range(8):
                    s.emit("vector", lambda v, g=g, w0=w0: v.tensor_copy(
                        out=BPACK[:, 0:512].rearrange("p (j gg) -> p j gg", gg=8)[:, :, g],
                        in_=QKVO[:, T * g + w0: T * g + w0 + WIN]),
                        waits=(bw if g == 0 else []))
                for g in range(8):
                    s.emit("vector", lambda v, g=g, w0=w0: v.tensor_copy(
                        out=BPACK[:, 512:1024].rearrange("p (j gg) -> p j gg", gg=8)[:, :, g],
                        in_=QKVO[:, T * (8 + g) + w0: T * (8 + g) + w0 + WIN]))
                for g in range(8):
                    s.emit("vector", lambda v, g=g, w0=w0: v.tensor_copy(
                        out=BPACK[:, 1024:1600].rearrange("p (j cc) -> p j cc", cc=9)[:, :, g],
                        in_=QKVO[:, T * (16 + g) + w0: T * (16 + g) + w0 + WIN]))
                mid["bld"] = s.c("cDVE")

                for j in range(16):
                    if cupto < 2:
                        continue
                    pj = j % 2
                    ovbank = 2 + (1 if j >= 8 else 0)
                    for t4 in range(4):
                        tok = 4 * j + t4
                        s.emit("tensor", lambda t, pj=pj, t4=t4, tok=tok: t.transpose(
                            PTB[pj][0:8, 256 * t4:256 * t4 + 128],
                            BPACK[:, 8 * tok:8 * tok + 8], CEYEB[:, :]),
                            waits=([("cDVE", max(mid["bld"], mid["bqkev"][pj]))]
                                   if t4 == 0 else []))
                        s.emit("tensor", lambda t, pj=pj, t4=t4, tok=tok: t.transpose(
                            PTB[pj][0:8, 256 * t4 + 128:256 * t4 + 256],
                            BPACK[:, 512 + 8 * tok:512 + 8 * tok + 8], CEYEB[:, :]),
                            inc=("cPE" if t4 == 3 else None))
                    mid["btr"][pj] = s.c("cPE")
                    if cupto < 3:
                        continue
                    s.emit("vector", lambda v, pj=pj: v.tensor_copy(
                        out=BQKT[0:8, 1024 * pj:1024 * pj + 1024],
                        in_=PTB[pj][0:8, 0:1024]),
                        waits=[("cPE", mid["btr"][pj]), ("cPE", mid["mm1"][pj]),
                               ("dST", st["store_done"])],
                        inc="cDVE")
                    mid["bqkev"][pj] = s.c("cDVE")
                    if cupto < 4:
                        continue
                    for t4 in range(4):
                        s.emit("tensor", lambda t, pj=pj, t4=t4: t.matmul(
                            PS[pj][:, 128 * t4:128 * t4 + 128],
                            BQKT[0:8, 1024 * pj + 256 * t4:1024 * pj + 256 * t4 + 128],
                            BQKT[0:8, 1024 * pj + 256 * t4 + 128:1024 * pj + 256 * t4 + 256],
                            start=True, stop=True),
                            waits=([("cDVE", max(mid["bqkev"][pj], mid["nmx"][pj])),
                                    ("cACT", mid["exp"][pj])] if t4 == 0 else []),
                            inc=("cPE" if t4 == 3 else None))
                    mid["mm1"][pj] = s.c("cPE")
                    if cupto < 5:
                        continue
                    s.emit("vector", lambda v, pj=pj: v.reduce_max(
                        out=NM[:, pj:pj + 1], in_=PS[pj][:, 0:512], axis=AX, negate=True),
                        waits=[("cPE", mid["mm1"][pj])], inc="cDVE")
                    mid["nmx"][pj] = s.c("cDVE")
                    if cupto < 6:
                        continue
                    s.emit("scalar", lambda sc, pj=pj: sc.activation(
                        out=EBUF[:, 512 * pj:512 * pj + 512], in_=PS[pj][:, 0:512],
                        func=AF.Exp, bias=NM[:, pj:pj + 1]),
                        waits=[("cDVE", mid["nmx"][pj]), ("cPE", mid["etr"][pj]),
                               ("dST", st["store_done"])],
                        inc="cACT")
                    mid["exp"][pj] = s.c("cACT")
                    if cupto < 7:
                        continue
                    for t4 in range(4):
                        s.emit("tensor", lambda t, pj=pj, t4=t4: t.transpose(
                            PTB[2 + pj][:, 128 * t4:128 * t4 + 128],
                            EBUF[:, 512 * pj + 128 * t4:512 * pj + 128 * t4 + 128],
                            CEYEB[:, :]),
                            waits=([("cACT", max(mid["exp"][pj], mid["etev"][pj]))]
                                   if t4 == 0 else []),
                            inc=("cPE" if t4 == 3 else None))
                    mid["etr"][pj] = s.c("cPE")
                    s.emit("scalar", lambda sc, pj=pj: sc.copy(
                        out=BET[:, 512 * pj:512 * pj + 512],
                        in_=PTB[2 + pj][:, 0:512]),
                        waits=[("cPE", max(mid["etr"][pj], mid["mm3"][pj])),
                               ("dST", st["store_done"])],
                        inc="cACT")
                    mid["etev"][pj] = s.c("cACT")
                    if cupto < 8:
                        continue
                    for t4 in range(4):
                        tl = (4 * j + t4) % 32
                        s.emit("tensor", lambda t, pj=pj, t4=t4, tl=tl, ob=ovbank, j=j: t.matmul(
                            PS[ob][:, 16 * tl:16 * tl + 9],
                            BET[:, 512 * pj + 128 * t4:512 * pj + 128 * t4 + 128],
                            BPACK[:, 1024 + 9 * (4 * j + t4):1024 + 9 * (4 * j + t4) + 9],
                            start=True, stop=True),
                            waits=([("cACT", mid["etev"][pj]),
                                    ("cDVE", mid["norm"][ovbank])] if t4 == 0 else []),
                            inc=("cPE" if t4 == 3 else None))
                    mid["mm3"][pj] = s.c("cPE")
                    if cupto < 9:
                        continue
                    if j % 8 == 7:
                        hb = j // 8
                        ob = 2 + hb
                        tok0 = w0 + 32 * hb
                        wmm3 = s.c("cPE")
                        s.emit("vector", lambda v, hb=hb, ob=ob: v.reciprocal(
                            out=RW[:, 32 * hb:32 * hb + 32], in_=PS[ob][:, 8:512:16]),
                            waits=[("cPE", wmm3)], inc="cDVE")
                        wrec = s.c("cDVE")
                        for g in range(8):
                            s.emit("vector", lambda v, g=g, hb=hb, ob=ob, tok0=tok0:
                                   v.tensor_tensor(
                                       out=QKVO[:, OVN0 + 8 * tok0 + g:
                                                OVN0 + 8 * tok0 + g + 8 * 31 + 1:8],
                                       in0=PS[ob][:, g:512:16],
                                       in1=RW[:, 32 * hb:32 * hb + 32], op=ALU.mult),
                                   waits=([("cDVE", wrec)] if g == 0 else []),
                                   inc=("cDVE" if g == 7 else None))
                        mid["norm"][ob] = s.c("cDVE")

        def phase_D(c):
            wnorm = max(mid["norm"][2], mid["norm"][3])
            for h in range(FT):
                g = h >> 2
                s.emit("vector", lambda v, h=h, g=g: v.tensor_add(
                    res_t(h), res_t(h), QKVO[:, OVN0 + g:OVN0 + g + 8 * (T - 1) + 1:8]),
                    waits=([("cDVE", wnorm), ("dST", st["store_done"])] if h == 0 else []),
                    inc=("cDVE" if h == FT - 1 else None))
            st["merge_done"] = s.c("cDVE")
            ln_stats(c, which=1)
            ln_apply1()
            st["ln1_done"] = s.c("cDVE")

        def ln_stats(c, which):
            wsrc = s.c("cDVE")
            wdpw = p1_pw if (c == 0 and which == 1) else 0
            sq_free = [0, 0]
            for h in range(FT):
                par = h % 2
                waits = []
                if h == 0:
                    waits += [("cDVE", wsrc)]
                    if wdpw:
                        waits += [("dPWA", wdpw)]
                if sq_free[par]:
                    waits += [("cPE", sq_free[par])]
                s.emit("vector", lambda v, h=h, par=par: v.tensor_mul(
                    MIDQ[:, 512 * par:512 * par + 512], res_t(h), res_t(h)),
                    waits=waits, inc="cDVE")
                wsq = s.c("cDVE")
                s.emit("tensor", lambda t, h=h: t.matmul(
                    PS[2][0:1, 0:T], CONESB[:, :], res_t(h),
                    start=(h == 0), stop=(h == FT - 1)),
                    waits=[("cDVE", wsq)] + ([("cACT", st["res_ready"])] if h == 0 else []))
                s.emit("tensor", lambda t, h=h, par=par: t.matmul(
                    PS[3][0:1, 0:T], CONESB[:, :], MIDQ[:, 512 * par:512 * par + 512],
                    start=(h == 0), stop=(h == FT - 1)),
                    inc="cPE")
                sq_free[par] = s.c("cPE")
            wstat = s.c("cPE")
            steps = [
                lambda v: v.tensor_scalar_mul(SVEC[0:1, 0:512], PS[2][0:1, 0:T], 1.0 / D),
                lambda v: v.tensor_scalar_mul(SVEC[0:1, 512:1024], PS[3][0:1, 0:T], 1.0 / D),
                lambda v: v.tensor_mul(SVEC[0:1, 1024:1536], SVEC[0:1, 0:512], SVEC[0:1, 0:512]),
                lambda v: v.tensor_sub(SVEC[0:1, 1536:2048], SVEC[0:1, 512:1024], SVEC[0:1, 1024:1536]),
                lambda v: v.tensor_scalar_add(SVEC[0:1, 2048:2560], SVEC[0:1, 1536:2048], EPS),
                lambda v: v.reciprocal(SVEC[0:1, 2560:3072], SVEC[0:1, 2048:2560]),
            ]
            wprev = wstat
            for i, fstep in enumerate(steps):
                s.emit("vector", fstep,
                       waits=([("cPE", wstat)] if i == 0 else [("cDVE", wprev)]),
                       inc="cDVE")
                wprev = s.c("cDVE")
            s.emit("scalar", lambda sc: sc.activation(
                out=SVEC[0:1, 1024:1536], in_=SVEC[0:1, 2560:3072], func=AF.Sqrt),
                waits=[("cDVE", wprev)], inc="cACT")
            wrstd = s.c("cACT")
            s.emit("tensor", lambda t: t.matmul(
                PS[2][:, 0:T], CONESF[:, :], SVEC[0:1, 0:512], start=True, stop=True),
                waits=[("cACT", wrstd)])
            s.emit("tensor", lambda t: t.matmul(
                PS[3][:, 0:T], CONESF[:, :], SVEC[0:1, 1024:1536], start=True, stop=True),
                inc="cPE")
            return s.c("cPE")

        def ln_apply1():
            wbc = s.c("cPE")
            for h in range(FT):
                par = h % 2
                s.emit("vector", lambda v, h=h, par=par: v.tensor_sub(
                    GSIL[:, 512 * par:512 * par + 512], res_t(h), PS[2][:, 0:T]),
                    waits=([("cPE", wbc), ("dST", st["store_done"])]
                           if h == 0 else []), inc="cDVE")
                s.emit("vector", lambda v, h=h, par=par: v.tensor_mul(
                    res_t(h), GSIL[:, 512 * par:512 * par + 512], PS[3][:, 0:T]),
                    inc="cDVE")

        def phase_q_gu(c, q):
            n, s0 = QN[q], QS0[q]
            blocks = []
            b0 = 0
            while b0 < n:
                blocks.append((b0, min(2, n - b0)))
                b0 += 2
            wld = {}
            gu_mm_done = {}
            mm_done = []

            def emit_loads(i):
                blk0, bn = blocks[i]
                par = i % 2
                sl0 = s0 + blk0
                for mat, mTb, buf, wfsem in (("g", wgT_b, WBUF, f"dWFg{par}"),
                                             ("u", wuT_b, QKVO, f"dWFu{par}")):
                    need = gT_done[(mat, q)]
                    wfree_pe = mm_done[i - 2] if i >= 2 else (
                        st["wbuf_free"] if mat == "g" else st["qkvo_free_pe"])
                    extra = []
                    if mat == "u" and i < 2:
                        extra = [("cDVE", st["ln1_done"]), ("dST", st["store_done"])]
                    s.emit("sync", lambda sy, mTb=mTb, buf=buf, sl0=sl0, bn=bn, par=par:
                           sy.dma_start(
                               out=buf[:, 8192 * par:8192 * par + 256 * FT]
                               .rearrange("p (hh j) -> p hh j", j=256)[:, :, 0:128 * bn],
                               in_=mTb[:, P * sl0:P * sl0 + P * bn]
                               .rearrange("(hh p) j -> p hh j", p=P)),
                           waits=[("dPWCgu", need), ("cPE", wfree_pe)] + extra,
                           inc=wfsem, amt=16, dma=True)
                    wld[(mat, i)] = s.c(wfsem)

            def emit_compute(i):
                blk0, bn = blocks[i]
                par2 = i % 2
                sl0 = s0 + blk0
                for bi in range(bn):
                    sl = sl0 + bi
                    par = sl % 2
                    for h in range(FT):
                        waits = [(f"dWFg{par2}", wld[("g", i)])]
                        if h == 0:
                            waits += [("cDVE", max(st["ln1_done"], st["dn_free"])),
                                      ("cACT", st["sil_free"][par])]
                        s.emit("tensor", lambda t, h=h, bi=bi, par=par, par2=par2: t.matmul(
                            PS[par][:, 0:T],
                            WBUF[:, 8192 * par2 + 256 * h + 128 * bi:
                                 8192 * par2 + 256 * h + 128 * bi + 128],
                            res_t(h), start=(h == 0), stop=(h == FT - 1)),
                            waits=waits, inc=("cPE" if h == FT - 1 else None))
                    wgmm = s.c("cPE")
                    s.emit("scalar", lambda sc, par=par: sc.activation(
                        out=GSIL[:, 512 * par:512 * par + 512], in_=PS[par][:, 0:T],
                        func=AF.Silu, scale=CSCL[:, 0:1]),
                        waits=[("cPE", wgmm), ("cDVE", st["mid_free"][par])], inc="cACT")
                    st["sil_free"][par] = s.c("cACT")
                    wsil = s.c("cACT")
                    for h in range(FT):
                        waits = [(f"dWFu{par2}", wld[("u", i)])]
                        if h == 0:
                            waits += [("cACT", wsil),
                                      ("cDVE", max(st["mid_free"][par], st["dn_free"]))]
                        s.emit("tensor", lambda t, h=h, bi=bi, par=par, par2=par2: t.matmul(
                            PS[2 + par][:, 0:T],
                            QKVO[:, 8192 * par2 + 256 * h + 128 * bi:
                                 8192 * par2 + 256 * h + 128 * bi + 128],
                            res_t(h), start=(h == 0), stop=(h == FT - 1)),
                            waits=waits, inc=("cPE" if h == FT - 1 else None))
                    wumm = s.c("cPE")
                    mq = sl - s0
                    s.emit("vector", lambda v, mq=mq, par=par: v.scalar_tensor_tensor(
                        out=MIDQ[:, 512 * mq:512 * mq + 512], in0=PS[2 + par][:, 0:T],
                        scalar=CSCL[:, 1:2], in1=GSIL[:, 512 * par:512 * par + 512],
                        op0=ALU.mult, op1=ALU.mult),
                        waits=[("cPE", wumm)], inc="cDVE")
                    st["mid_free"][par] = s.c("cDVE")
                mm_done.append(s.c("cPE"))

            for i in range(len(blocks)):
                emit_loads(i)
                if i >= 1:
                    emit_compute(i - 1)
            emit_compute(len(blocks) - 1)
            st["wbuf_free"] = s.c("cPE")
            st["qkvo_free_pe"] = s.c("cPE")
            return s.c("cDVE")

        def phase_q_down(c, q, wmid):
            n, s0 = QN[q], QS0[q]
            for fg in range(8):
                sub = []
                for sb0 in range(0, n, 8):
                    sub.append((sb0, min(8, n - sb0)))
                for (sb0, sbn) in sub:
                    par8 = (sb0 // 8) % 2
                    slot = WDN[:, 4096 * par8:4096 * par8 + 512 * sbn]
                    need = dT_done[s0 + sb0 + sbn - 1]
                    wds = "dWD0" if par8 == 0 else "dWD1"
                    s.emit("sync", lambda sy, s0=s0, sb0=sb0, sbn=sbn, fg=fg, sl_=slot:
                           sy.dma_start(
                               out=sl_.rearrange("p (ss j) -> p ss j", j=512),
                               in_=wdT_b[P * (s0 + sb0):P * (s0 + sb0) + P * sbn,
                                         512 * fg:512 * fg + 512]
                               .rearrange("(ss p) j -> p ss j", p=P)),
                           waits=[("dPWCwd", need), ("cPE", st["wdn_free"][par8]),
                                  ("dST", st["store_done"])],
                           inc=wds, amt=16, dma=True)
                    wldd = s.c(wds)
                    for si in range(sbn):
                        sl = s0 + sb0 + si
                        first = (sb0 + si == 0)
                        last = (sb0 + si == n - 1)
                        for tf in range(4):
                            waits = [(wds, wldd)]
                            if first and tf == 0:
                                waits += [("cDVE", max(wmid, st["dn_free"]))]
                                if c == 0 and q == 0 and fg == 0:
                                    waits += [("dPWCgu", p3_done[0]),
                                              ("dPWCwd", p3_done[1])]
                            s.emit("tensor", lambda t, si=si, tf=tf, sl=sl, par8=par8,
                                   s0=s0, first=first, last=last: t.matmul(
                                       PS[tf][:, 0:T],
                                       WDN[:, 4096 * par8 + 512 * si + 128 * tf:
                                           4096 * par8 + 512 * si + 128 * tf + 128],
                                       MIDQ[:, 512 * (sl - s0):512 * (sl - s0) + 512],
                                       start=first, stop=last),
                                   waits=waits,
                                   inc=("cPE" if tf == 3 and si == sbn - 1 else None))
                    st["wdn_free"][par8] = s.c("cPE")
                wdn = s.c("cPE")
                for tf in range(4):
                    h = 4 * fg + tf
                    pslice = PART[:, T * h:T * h + T]
                    if q == 0:
                        s.emit("vector", lambda v, tf=tf, ps=pslice: v.tensor_scalar(
                            out=ps, in0=PS[tf][:, 0:T], scalar1=CSCL[:, 2:3],
                            scalar2=None, op0=ALU.mult),
                            waits=([("cPE", wdn)] if tf == 0 else []),
                            inc=("cDVE" if tf == 3 else None))
                    elif q < 3:
                        s.emit("vector", lambda v, tf=tf, ps=pslice: v.scalar_tensor_tensor(
                            out=ps, in0=PS[tf][:, 0:T], scalar=CSCL[:, 2:3], in1=ps,
                            op0=ALU.mult, op1=ALU.add),
                            waits=([("cPE", wdn)] if tf == 0 else []),
                            inc=("cDVE" if tf == 3 else None))
                    else:
                        s.emit("vector", lambda v, tf=tf, ps=pslice: v.scalar_tensor_tensor(
                            out=GSIL[:, 0:512], in0=PS[tf][:, 0:T], scalar=CSCL[:, 2:3],
                            in1=ps, op0=ALU.mult, op1=ALU.add),
                            waits=([("cPE", wdn), ("dST", st["store_done"])]
                                   if tf == 0 else []), inc="cDVE")
                        wtmp = s.c("cDVE")
                        s.emit("vector", lambda v, h=h: v.tensor_add(
                            res_t(h), res_t(h), GSIL[:, 0:512]),
                            waits=[("cDVE", wtmp)], inc="cDVE")
                st["dn_free"] = s.c("cDVE")

        def phase_out(c):
            wbc = ln_stats(c, which=2)
            ev_free = [0, 0]
            prev_store = 0
            for half in range(2):
                for h in range(FT):
                    waits = []
                    if h == 0:
                        waits += [("cPE", wbc)]
                        if half == 1:
                            waits += [("cPE", st["out_tr"])]
                    s.emit("vector", lambda v, h=h, half=half: v.tensor_sub(
                        GSIL[:, 512:768], res_t(h)[:, 256 * half:256 * half + 256],
                        PS[2][:, 256 * half:256 * half + 256]),
                        waits=waits, inc="cDVE")
                    s.emit("vector", lambda v, h=h, half=half: v.tensor_mul(
                        h2f[:, 256 * h:256 * h + 256], GSIL[:, 512:768],
                        PS[3][:, 256 * half:256 * half + 256]),
                        inc="cDVE")
                wapp = s.c("cDVE")
                for tt in (2 * half, 2 * half + 1):
                    for h in range(FT):
                        bank = PS[h % 2]
                        waits = [("cACT", ev_free[h % 2])] if ev_free[h % 2] else []
                        if h == 0:
                            waits += [("cDVE", wapp)]
                            if prev_store:
                                waits += [("dST", prev_store)]
                        s.emit("tensor", lambda t, h=h, tt=tt, b=bank: t.transpose(
                            b[:, 0:P],
                            h2f[:, 256 * h + 128 * (tt % 2):256 * h + 128 * (tt % 2) + 128],
                            CEYEF[:, :]),
                            waits=waits, inc="cPE")
                        wtr = s.c("cPE")
                        s.emit("scalar", lambda sc, h=h, b=bank: sc.copy(
                            out=ostf[:, h:4096:32], in_=b[:, 0:P]),
                            waits=[("cPE", wtr)], inc="cACT")
                        ev_free[h % 2] = s.c("cACT")
                    wev = s.c("cACT")
                    s.emit("sync", lambda sy, c=c, tt=tt: sy.dma_start(
                        out=y_out[T * c + P * tt:T * c + P * tt + P, :], in_=ostf[:, :]),
                        waits=[("cACT", wev)], inc="dST", amt=16, dma=True)
                    prev_store = s.c("dST")
                st["out_tr"] = s.c("cPE")
            st["store_done"] = s.c("dST")

        def dbg_dump(name, src_ap, waits):
            if dbg:
                s.emit("sync", lambda sy, d=dbg_t[name], sp=src_ap: sy.dma_start(
                    out=d[:, 0:sp.shape[1]], in_=sp), waits=waits,
                    inc="dST", amt=16, dma=True)
                st["store_done"] = s.c("dST")

        p3_done = [0, 0]
        for c in range(nchunk):
            if upto < 1:
                break
            phase_A(c)
            if c == 0:
                dbg_dump("d_x", RES[:, :], [("cACT", st["res_ready"])])
            if upto < 2:
                break
            phase_B(c)
            if c == 0:
                dbg_dump("d_qkv", QKVO[:, 0:12288], [("cDVE", st["qkv_evac"])])
            if upto < 3:
                break
            phase_C(c)
            if c == 0:
                dbg_dump("d_ovn", QKVO[:, 12288:16384],
                         [("cDVE", max(mid["norm"][2], mid["norm"][3]))])
            if upto < 4:
                break
            phase_D(c)
            if c == 0:
                p3_done[0], p3_done[1] = emit_P3()
                dbg_dump("d_h1", RES[:, :], [("cDVE", st["ln1_done"])])
            if upto < 5:
                break
            for q in range(4):
                wmid = phase_q_gu(c, q)
                if upto < 6:
                    break
                phase_q_down(c, q, wmid)
            if c == 0:
                dbg_dump("d_h2", RES[:, :], [("cDVE", st["dn_free"])])
            if upto < 7:
                break
            phase_out(c)

        final_st = s.c("dST")

        def replay(eng_obj, name):
            last = {}
            for ws, fn, inc, amt in s.ops[name]:
                for sm, v in ws:
                    if v > last.get(sm, -1):
                        eng_obj.wait_ge(sems[sm], v)
                        last[sm] = v
                ins = fn(eng_obj)
                if inc is not None:
                    ins.then_inc(sems[inc], amt)
            if name == "sync":
                eng_obj.wait_ge(sems["dST"], final_st)

        @block.sync
        def _(sync):
            replay(sync, "sync")

        @block.scalar
        def _(scalar):
            replay(scalar, "scalar")

        @block.vector
        def _(vector):
            replay(vector, "vector")

        @block.tensor
        def _(tensor):
            replay(tensor, "tensor")

        @block.gpsimd
        def _(gpsimd):
            replay(gpsimd, "gpsimd")

    return nc


def kernel(L0_attn_x, L0_attn_Wq, L0_attn_bq, L0_attn_Wk, L0_attn_bk,
           L0_attn_Wv, L0_attn_bv, L0_ffn_w_gate, L0_ffn_w_up, L0_ffn_w_down,
           L0_ffn_scale_gate, L0_ffn_scale_up, L0_ffn_scale_down):
    import ml_dtypes
    x = np.ascontiguousarray(np.asarray(L0_attn_x, dtype=np.float32).reshape(16384, D))
    wq_h = np.ascontiguousarray(np.asarray(L0_attn_Wq, np.float32).reshape(D, 4096))
    wk_h = np.ascontiguousarray(np.asarray(L0_attn_Wk, np.float32).reshape(D, 1024))
    wv_h = np.ascontiguousarray(np.asarray(L0_attn_Wv, np.float32).reshape(D, 1024))
    bq_h = np.ascontiguousarray(np.asarray(L0_attn_bq, np.float32))
    bk_h = np.ascontiguousarray(np.asarray(L0_attn_bk, np.float32))
    bv_h = np.ascontiguousarray(np.asarray(L0_attn_bv, np.float32))
    wg_h = np.ascontiguousarray(np.asarray(L0_ffn_w_gate, np.int32))
    wu_h = np.ascontiguousarray(np.asarray(L0_ffn_w_up, np.int32))
    wd_h = np.ascontiguousarray(np.asarray(L0_ffn_w_down, np.int32))
    scl_h = np.ascontiguousarray(np.broadcast_to(
        np.stack([np.asarray(L0_ffn_scale_gate, np.float32).reshape(()),
                  np.asarray(L0_ffn_scale_up, np.float32).reshape(()),
                  np.asarray(L0_ffn_scale_down, np.float32).reshape(())]).reshape(1, 3),
        (P, 3)))
    eyef_h = np.eye(P, dtype=np.float32)
    eyeb_h = np.eye(P).astype(ml_dtypes.bfloat16)
    onesb_h = np.ones((P, 1)).astype(ml_dtypes.bfloat16)
    onesf_h = np.ones((1, P), dtype=np.float32)

    nc = build_nc()
    in_maps = []
    for cid in range(NCORE):
        in_maps.append({
            "x_in": np.ascontiguousarray(x[TOK_CORE * cid: TOK_CORE * (cid + 1)]),
            "wq": wq_h, "wk": wk_h, "wv": wv_h,
            "bq": bq_h, "bk": bk_h, "bv": bv_h,
            "wg": wg_h, "wu": wu_h, "wd": wd_h,
            "scl": scl_h, "eyef": eyef_h, "eyeb": eyeb_h,
            "onesb": onesb_h, "onesf": onesf_h,
        })
    res = run_bass_kernel_spmd(nc, in_maps, core_ids=list(range(NCORE)))
    out = np.concatenate([r["y_out"] for r in res.results], axis=0)
    return out.reshape(2, 8192, D)


# version 03
# speedup vs baseline: 1.1963x; 1.1058x over previous
import numpy as np
from contextlib import ExitStack

import concourse.bass as bass
import concourse.mybir as mybir
from concourse.bass_utils import run_bass_kernel_spmd

F32 = mybir.dt.float32
BF16 = mybir.dt.bfloat16
I32 = mybir.dt.int32
ALU = mybir.AluOpType
AF = mybir.ActivationFunctionType
AX = mybir.AxisListType.X

P = 128
D = 4096
HID = 11008
NSLAB = 86
FT = 32
T = 512
NCHUNK = 4
TOK_CORE = T * NCHUNK
NCORE = 8
WIN = 64
RSQ = float(1.0 / np.sqrt(128.0))
EPS = 1e-5
QN = [22, 22, 21, 21]
QS0 = [0, 22, 44, 65]
OVN0 = 12288


class Sched:
    ENGINES = ("sync", "scalar", "vector", "tensor", "gpsimd")

    def __init__(self):
        self.ops = {e: [] for e in self.ENGINES}
        self.cnt = {}

    ENG_SEM = {"vector": "cDVE", "scalar": "cACT", "gpsimd": "cPOOL"}

    def emit(self, eng, fn, waits=(), inc=None, amt=1, dma=False):
        ws = []
        if not dma and eng in self.ENG_SEM:
            osem = self.ENG_SEM[eng]
            prev = self.cnt.get(osem, 0)
            waits = list(waits) + [(osem, prev)]
            assert inc is None or inc == osem, f"{eng} compute op inc={inc}"
            inc, amt = osem, 1
        for sm, v in waits:
            if v > 0:
                assert v <= self.cnt.get(sm, 0), \
                    f"wait on {sm}>={v} but only {self.cnt.get(sm, 0)} emitted"
                ws.append((sm, v))
        self.ops[eng].append((ws, fn, inc, amt))
        if inc is not None:
            self.cnt[inc] = self.cnt.get(inc, 0) + amt
        return self.cnt.get(inc, 0) if inc else None

    def c(self, sem):
        return self.cnt.get(sem, 0)


def build_nc(nchunk=NCHUNK, dbg=False, upto=99, cupto=99):
    nc = bass.Bass("TRN2", target_bir_lowering=False)
    tok_core = T * nchunk

    x_in = nc.dram_tensor("x_in", [tok_core, D], F32, kind="ExternalInput").ap()
    wq = nc.dram_tensor("wq", [D, 4096], F32, kind="ExternalInput").ap()
    wk = nc.dram_tensor("wk", [D, 1024], F32, kind="ExternalInput").ap()
    wv = nc.dram_tensor("wv", [D, 1024], F32, kind="ExternalInput").ap()
    bq = nc.dram_tensor("bq", [P, 32], F32, kind="ExternalInput").ap()
    bk = nc.dram_tensor("bk", [P, 8], F32, kind="ExternalInput").ap()
    bv = nc.dram_tensor("bv", [P, 8], F32, kind="ExternalInput").ap()
    wg = nc.dram_tensor("wg", [HID, D], I32, kind="ExternalInput").ap()
    wu = nc.dram_tensor("wu", [HID, D], I32, kind="ExternalInput").ap()
    wd = nc.dram_tensor("wd", [D, HID], I32, kind="ExternalInput").ap()
    scl = nc.dram_tensor("scl", [P, 3], F32, kind="ExternalInput").ap()
    eyef = nc.dram_tensor("eyef", [P, P], F32, kind="ExternalInput").ap()
    eyeb = nc.dram_tensor("eyeb", [P, P], BF16, kind="ExternalInput").ap()
    onesb = nc.dram_tensor("onesb", [P, 1], BF16, kind="ExternalInput").ap()
    onesf = nc.dram_tensor("onesf", [1, P], F32, kind="ExternalInput").ap()
    y_out = nc.dram_tensor("y_out", [tok_core, D], F32, kind="ExternalOutput").ap()

    wattn_b = nc.dram_tensor("wattn_b", [D, 3072], BF16, kind="Internal").ap()
    wg_b = nc.dram_tensor("wg_b", [HID, D], BF16, kind="Internal").ap()
    wu_b = nc.dram_tensor("wu_b", [HID, D], BF16, kind="Internal").ap()
    wd_b = nc.dram_tensor("wd_b", [D, HID], BF16, kind="Internal").ap()
    wgT_b = nc.dram_tensor("wgT_b", [D, HID], BF16, kind="Internal").ap()
    wuT_b = nc.dram_tensor("wuT_b", [D, HID], BF16, kind="Internal").ap()
    wdT_b = nc.dram_tensor("wdT_b", [HID, D], BF16, kind="Internal").ap()

    dbg_t = {}
    if dbg:
        for name in ("d_x", "d_qkv", "d_ovn", "d_h1", "d_h2"):
            dbg_t[name] = nc.dram_tensor(name, [P, 16384], BF16, kind="ExternalOutput").ap()
        for name in ("d_bqkt", "d_ebuf", "d_bet", "d_bpack"):
            dbg_t[name] = nc.dram_tensor(name, [P, 4672], BF16, kind="ExternalOutput").ap()
        dbg_t["d_pov"] = nc.dram_tensor("d_pov", [P, 512], F32, kind="ExternalOutput").ap()

    s = Sched()

    with ExitStack() as ctx:
        en = ctx.enter_context
        RES = en(nc.sbuf_tensor("RES", [P, FT * T], BF16))
        QKVO = en(nc.sbuf_tensor("QKVO", [P, 16384], BF16))
        WBUF = en(nc.sbuf_tensor("WBUF", [P, 16384], BF16))
        MIDQ = en(nc.sbuf_tensor("MIDQ", [P, 11264], BF16))
        PART = en(nc.sbuf_tensor("PART", [P, 16384], BF16))
        BPACK = en(nc.sbuf_tensor("BPACK", [P, 1600], BF16))
        OSTG = en(nc.sbuf_tensor("OSTG", [P, 8192], BF16))
        WATT = en(nc.sbuf_tensor("WATT", [P, 1024], BF16))
        EBUF = en(nc.sbuf_tensor("EBUF", [P, 1024], BF16))
        BET = en(nc.sbuf_tensor("BET", [P, 1024], BF16))
        BQKT = en(nc.sbuf_tensor("BQKT", [P, 2048], BF16))
        GSIL = en(nc.sbuf_tensor("GSIL", [P, 1024], F32))
        NM = en(nc.sbuf_tensor("NM", [P, 2], F32))
        RW = en(nc.sbuf_tensor("RW", [P, 64], F32))
        SVEC = en(nc.sbuf_tensor("SVEC", [1, 3072], F32))
        CEYEF = en(nc.sbuf_tensor("CEYEF", [P, P], F32))
        CEYEB = en(nc.sbuf_tensor("CEYEB", [P, P], BF16))
        CONESB = en(nc.sbuf_tensor("CONESB", [P, 1], BF16))
        CONESF = en(nc.sbuf_tensor("CONESF", [1, P], F32))
        CBQ = en(nc.sbuf_tensor("CBQ", [P, 48], F32))
        CBK = en(nc.sbuf_tensor("CBK", [P, 8], F32))
        CBV = en(nc.sbuf_tensor("CBV", [P, 8], F32))
        CSCL = en(nc.sbuf_tensor("CSCL", [P, 3], F32))
        CBQS = en(nc.sbuf_tensor("CBQS", [P, 8], F32))

        PS = [en(nc.psum_tensor(f"PS{i}", [P, 512], F32)) for i in range(4)]
        PTB = [en(nc.psum_tensor(f"PTB{i}", [P, 1024], BF16)) for i in range(4)]

        sem_names = ["dX0", "dX1", "dWA", "dWA0", "dWA1", "dWFg0", "dWFg1", "dWFu0", "dWFu1",
                     "dWD0", "dWD1", "dCast",
                     "dPWA", "gRd0", "gRd1", "gPW0", "gPW1",
                     "uRd0", "uRd1", "uPW0", "uPW1",
                     "wRd0", "wRd1", "wPW0", "wPW1",
                     "dST", "dMisc", "cPE", "cACT", "cDVE", "cPOOL"]
        sems = {n: en(nc.semaphore(n)) for n in sem_names}
        block = en(nc.Block())

        xstage = WBUF[:, :].bitcast(F32)
        partf = PART[:, :].bitcast(F32)
        h2f = PART[:, :].bitcast(F32)
        ostf = OSTG[:, :].bitcast(F32)
        WDN = OSTG

        def res_t(h):
            return RES[:, T * h: T * h + T]

        for dst, src in [(CEYEF[:, :], eyef), (CEYEB[:, :], eyeb),
                         (CONESB[:, :], onesb), (CONESF[:, :], onesf),
                         (CBQ[:, 0:32], bq), (CBK[:, :], bk), (CBV[:, :], bv),
                         (CSCL[:, :], scl)]:
            s.emit("sync", lambda sy, d=dst, sr=src: sy.dma_start(out=d, in_=sr),
                   inc="dMisc", amt=16, dma=True)
        w_misc = s.c("dMisc")

        s.emit("vector", lambda v: v.tensor_add(
            CBQ[:, 32:48], CBQ[:, 0:32:2], CBQ[:, 1:32:2]),
            waits=[("dMisc", w_misc)])
        s.emit("vector", lambda v: v.tensor_add(
            CBQ[:, 0:8], CBQ[:, 32:48:2], CBQ[:, 33:48:2]))
        s.emit("vector", lambda v: v.tensor_scalar_mul(CBQS[:, :], CBQ[:, 0:8], RSQ))
        s.emit("vector", lambda v: v.memset(BPACK[:, 0:1600], 0.0))
        s.emit("vector", lambda v: v.memset(
            BPACK[:, 1024:1600].rearrange("p (j c) -> p j c", c=9)[:, :, 8], 1.0))

        wq_src = wq.rearrange("(d h) c -> h d c", h=32)
        wk_src = wk.rearrange("(d h) c -> h d c", h=32)
        wv_src = wv.rearrange("(d h) c -> h d c", h=32)
        for h in range(FT):
            wprev = s.c("cDVE")
            s.emit("sync", lambda sy, h=h: sy.dma_start(out=partf[:, 0:4096], in_=wq_src[h, :, :]),
                   waits=[("cDVE", wprev)], inc="dWA", amt=16, dma=True)
            s.emit("sync", lambda sy, h=h: sy.dma_start(out=partf[:, 4096:5120], in_=wk_src[h, :, :]),
                   inc="dWA", amt=16, dma=True)
            s.emit("sync", lambda sy, h=h: sy.dma_start(out=partf[:, 5120:6144], in_=wv_src[h, :, :]),
                   inc="dWA", amt=16, dma=True)
            wload = s.c("dWA")
            wpw = s.c("dPWA")

            s.emit("vector", lambda v: v.tensor_add(
                partf[:, 6144:8192], partf[:, 0:4096:2], partf[:, 1:4096:2]),
                waits=[("dWA", wload), ("dPWA", wpw)])
            s.emit("vector", lambda v: v.tensor_add(
                partf[:, 0:1024], partf[:, 6144:8192:2], partf[:, 6145:8192:2]))
            s.emit("vector", lambda v: v.tensor_scalar(
                out=MIDQ[:, 0:1024].rearrange("p (g d) -> p g d", d=P),
                in0=partf[:, 0:1024].rearrange("p (d g) -> p g d", g=8),
                scalar1=RSQ, scalar2=None, op0=ALU.mult))
            s.emit("vector", lambda v: v.tensor_copy(
                out=MIDQ[:, 1024:2048].rearrange("p (g e) -> p g e", e=P),
                in_=partf[:, 4096:5120].rearrange("p (e g) -> p g e", g=8)))
            s.emit("vector", lambda v: v.tensor_copy(
                out=MIDQ[:, 2048:3072].rearrange("p (g e) -> p g e", e=P),
                in_=partf[:, 5120:6144].rearrange("p (e g) -> p g e", g=8)))
            wprep = s.c("cDVE")
            s.emit("scalar", lambda sc, h=h: sc.dma_start(
                out=wattn_b[P * h: P * h + P, :], in_=MIDQ[:, 0:3072]),
                waits=[("cDVE", wprep)], inc="dPWA", amt=16, dma=True)
        p1_pw = s.c("dPWA")
        p1_dve = s.c("cDVE")

        wd_src = wd.rearrange("(d h) c -> h d c", h=32)
        wd_dst = wd_b.rearrange("(h d) c -> h d c", d=P)
        SEG = 5632
        cast_done = {}
        s.emit("gpsimd", lambda g: g.dma_start(out=wg_b[0:SEG, :], in_=wg[0:SEG, :]),
               waits=[("dPWA", p1_pw)], inc="dCast", amt=16, dma=True)
        cast_done[("g", 0)] = cast_done[("g", 1)] = s.c("dCast")
        s.emit("gpsimd", lambda g: g.dma_start(out=wu_b[0:SEG, :], in_=wu[0:SEG, :]),
               waits=[("dCast", 16)], inc="dCast", amt=16, dma=True)
        cast_done[("u", 0)] = cast_done[("u", 1)] = s.c("dCast")
        s.emit("gpsimd", lambda g: g.dma_start(out=wd_dst[:, :, 0:SEG], in_=wd_src[:, :, 0:SEG]),
               waits=[("dCast", 32)], inc="dCast", amt=16, dma=True)
        cast_done[("d", 0)] = cast_done[("d", 1)] = s.c("dCast")
        s.emit("gpsimd", lambda g: g.dma_start(out=wd_dst[:, :, SEG:HID], in_=wd_src[:, :, SEG:HID]),
               waits=[("dCast", 48)], inc="dCast", amt=16, dma=True)
        cast_done[("d", 2)] = cast_done[("d", 3)] = s.c("dCast")
        s.emit("gpsimd", lambda g: g.dma_start(out=wg_b[SEG:HID, :], in_=wg[SEG:HID, :]),
               waits=[("dCast", 64)], inc="dCast", amt=16, dma=True)
        cast_done[("g", 2)] = cast_done[("g", 3)] = s.c("dCast")
        s.emit("gpsimd", lambda g: g.dma_start(out=wu_b[SEG:HID, :], in_=wu[SEG:HID, :]),
               waits=[("dCast", 80)], inc="dCast", amt=16, dma=True)
        cast_done[("u", 2)] = cast_done[("u", 3)] = s.c("dCast")

        QHB = [QN[i] * P for i in range(4)]
        QH0 = [QS0[i] * P for i in range(4)]
        gT_done = {}
        dT_done = {}

        def emit_P3():
            items = {"g": [], "u": [], "w": []}
            for q in range(4):
                for b in range(FT):
                    items["g"].append((q, b))
                    items["u"].append((q, b))
                for sl in range(QS0[q], QS0[q] + QN[q]):
                    for hf in range(2):
                        items["w"].append((q, sl, hf))
            slots = {"g": [PART[:, 0:2816], PART[:, 2816:5632]],
                     "u": [PART[:, 5632:8448], PART[:, 8448:11264]],
                     "w": [PART[:, 11264:13312], PART[:, 13312:15360]]}
            mats = {"g": (wg_b, wgT_b), "u": (wu_b, wuT_b)}
            prev_store = {}
            maxlen = max(len(v) for v in items.values())
            first_gate = [("cDVE", p1_dve)]
            for k in range(maxlen):
                for ch in ("g", "u", "w"):
                    if k >= len(items[ch]):
                        continue
                    par = k % 2
                    rsem = f"{'g' if ch == 'g' else 'u' if ch == 'u' else 'w'}Rd{par}"
                    psem = f"{'g' if ch == 'g' else 'u' if ch == 'u' else 'w'}PW{par}"
                    slot = slots[ch][par]
                    waits = list(first_gate) if k == 0 else []
                    if (ch, par) in prev_store:
                        waits.append((psem, prev_store[(ch, par)]))
                    if ch in ("g", "u"):
                        q, b = items[ch][k]
                        mb, mTb = mats[ch]
                        waits.append(("dCast", cast_done[(ch, q)]))
                        s.emit("sync", lambda sy, mb=mb, q=q, b=b, slot=slot: sy.dma_start(
                            out=slot[:, 0:QHB[q]],
                            in_=mb[QH0[q]:QH0[q] + QHB[q], P * b:P * b + P],
                            transpose=True),
                            waits=waits, inc=rsem, amt=16, dma=True)
                        wrd = s.c(rsem)
                        s.emit("scalar", lambda sc, mTb=mTb, q=q, b=b, slot=slot: sc.dma_start(
                            out=mTb.rearrange("(h x) c -> x h c", x=P)
                            [4 * b:4 * b + 4, :, QH0[q]:QH0[q] + QHB[q]],
                            in_=slot[:, 0:QHB[q]]),
                            waits=[(rsem, wrd)], inc=psem, amt=16, dma=True)
                    else:
                        q, sl, hf = items[ch][k]
                        waits.append(("dCast", cast_done[("d", q)]))
                        s.emit("sync", lambda sy, sl=sl, hf=hf, slot=slot: sy.dma_start(
                            out=slot[:, 0:2048],
                            in_=wd_b[2048 * hf:2048 * hf + 2048, P * sl:P * sl + P],
                            transpose=True),
                            waits=waits, inc=rsem, amt=16, dma=True)
                        wrd = s.c(rsem)
                        s.emit("scalar", lambda sc, sl=sl, hf=hf, slot=slot: sc.dma_start(
                            out=wdT_b[P * sl:P * sl + P, 2048 * hf:2048 * hf + 2048],
                            in_=slot[:, 0:2048]),
                            waits=[(rsem, wrd)], inc=psem, amt=16, dma=True)
                    prev_store[(ch, par)] = s.c(psem)
                    if ch in ("g", "u"):
                        if b == FT - 1:
                            gT_done[(("g" if ch == "g" else "u"), q)] = [
                                ("gPW0" if ch == "g" else "uPW0", s.c("gPW0" if ch == "g" else "uPW0")),
                                ("gPW1" if ch == "g" else "uPW1", s.c("gPW1" if ch == "g" else "uPW1"))]
                    else:
                        if hf == 1:
                            dT_done[sl] = [("wPW0", s.c("wPW0")), ("wPW1", s.c("wPW1"))]
            return ([("gPW0", s.c("gPW0")), ("gPW1", s.c("gPW1")),
                     ("uPW0", s.c("uPW0")), ("uPW1", s.c("uPW1"))],
                    [("wPW0", s.c("wPW0")), ("wPW1", s.c("wPW1"))])

        st = dict(
            store_done=0,
            xslot_free=[0, 0],
            res_ready=0,
            qkv_evac=0,
            wbuf_free=0,
            qkvo_free_pe=0,
            merge_done=0, ln1_done=0,
            sil_free=[0, 0],
            mid_free=[0, 0],
            wdn_free=[0, 0],
            dn_free=0,
            out_tr=0,
        )
        mid = dict(mm1=[0, 0], exp=[0, 0], nmx=[0, 0], etr=[0, 0], etev=[0, 0],
                   mm3=[0, 0], bqkev=[0, 0], btr=[0, 0], norm={2: 0, 3: 0}, bld=0)

        def phase_A(c):
            for tt in range(4):
                par = tt % 2
                slot = xstage[:, 4096 * par: 4096 * par + 4096]
                dxs = "dX0" if par == 0 else "dX1"
                s.emit("sync", lambda sy, c=c, tt=tt, sl=slot: sy.dma_start(
                    out=sl, in_=x_in[T * c + P * tt: T * c + P * (tt + 1), :]),
                    waits=[("cPE", st["xslot_free"][par]), ("dST", st["store_done"])],
                    inc=dxs, amt=16, dma=True)
                wx = s.c(dxs)
                for h in range(FT):
                    bank = PS[2 + (h % 2)]
                    wev = s.c("cACT")
                    s.emit("tensor", lambda t, sl=slot, h=h, b=bank, dxs=dxs: t.transpose(
                        b[:, 0:P], sl[:, h:4096:32], CEYEF[:, :]),
                        waits=[(dxs, wx), ("cACT", wev - 1)], inc="cPE")
                    wtr = s.c("cPE")
                    s.emit("scalar", lambda sc, h=h, tt=tt, b=bank: sc.copy(
                        out=res_t(h)[:, P * tt:P * (tt + 1)], in_=b[:, 0:P]),
                        waits=[("cPE", wtr)], inc="cACT")
                st["xslot_free"][par] = s.c("cPE")
            st["res_ready"] = s.c("cACT")
            st["wbuf_free"] = s.c("cPE")

        def phase_B(c):
            bank_free_dve = s.c("cDVE")
            for grp in range(6):
                grp_mm_base = s.c("cPE")
                for h in range(FT):
                    par = h % 2
                    ws = WATT[:, 512 * par: 512 * par + 512]
                    wfree = grp_mm_base + h - 1 if h >= 2 else grp_mm_base
                    was = "dWA0" if par == 0 else "dWA1"
                    s.emit("sync", lambda sy, grp=grp, h=h, ws=ws: sy.dma_start(
                        out=ws, in_=wattn_b[P * h: P * h + P, 512 * grp: 512 * grp + 512]),
                        waits=[("cPE", wfree), ("dPWA", 16 * (h + 1))], inc=was, amt=16, dma=True)
                    wld = s.c(was)
                    for m6 in range(4):
                        waits = [(was, wld)]
                        if h == 0:
                            waits += [("cACT", st["res_ready"]), ("cDVE", bank_free_dve)]
                        s.emit("tensor", lambda t, m6=m6, h=h, ws=ws: t.matmul(
                            PS[m6][:, 0:T], ws[:, 128 * m6:128 * m6 + 128], res_t(h),
                            start=(h == 0), stop=(h == FT - 1)),
                            waits=waits, inc=("cPE" if m6 == 3 else None))
                wmm = s.c("cPE")
                for m6 in range(4):
                    m = grp * 4 + m6
                    if m < 8:
                        bias = CBQS[:, m:m + 1]
                    elif m < 16:
                        bias = CBK[:, m - 8:m - 7]
                    else:
                        bias = CBV[:, m - 16:m - 15]
                    s.emit("vector", lambda v, m=m, m6=m6, bias=bias: v.tensor_scalar(
                        out=QKVO[:, T * m: T * m + T], in0=PS[m6][:, 0:T],
                        scalar1=bias, scalar2=None, op0=ALU.add),
                        waits=[("cPE", wmm)], inc="cDVE")
                bank_free_dve = s.c("cDVE")
            st["qkv_evac"] = s.c("cDVE")

        def phase_C(c):
            for w in range(T // WIN):
                w0 = WIN * w
                bw = [("cDVE", st["qkv_evac"]),
                      ("cPE", max(mid["btr"][0], mid["btr"][1],
                                  mid["mm3"][0], mid["mm3"][1])),
                      ("dST", st["store_done"])]
                for g in range(8):
                    s.emit("vector", lambda v, g=g, w0=w0: v.tensor_copy(
                        out=BPACK[:, 0:512].rearrange("p (j gg) -> p j gg", gg=8)[:, :, g],
                        in_=QKVO[:, T * g + w0: T * g + w0 + WIN]),
                        waits=(bw if g == 0 else []))
                for g in range(8):
                    s.emit("vector", lambda v, g=g, w0=w0: v.tensor_copy(
                        out=BPACK[:, 512:1024].rearrange("p (j gg) -> p j gg", gg=8)[:, :, g],
                        in_=QKVO[:, T * (8 + g) + w0: T * (8 + g) + w0 + WIN]))
                for g in range(8):
                    s.emit("vector", lambda v, g=g, w0=w0: v.tensor_copy(
                        out=BPACK[:, 1024:1600].rearrange("p (j cc) -> p j cc", cc=9)[:, :, g],
                        in_=QKVO[:, T * (16 + g) + w0: T * (16 + g) + w0 + WIN]))
                mid["bld"] = s.c("cDVE")

                for j in range(16):
                    if cupto < 2:
                        continue
                    pj = j % 2
                    ovbank = 2 + (1 if j >= 8 else 0)
                    for t4 in range(4):
                        tok = 4 * j + t4
                        s.emit("tensor", lambda t, pj=pj, t4=t4, tok=tok: t.transpose(
                            PTB[pj][0:8, 256 * t4:256 * t4 + 128],
                            BPACK[:, 8 * tok:8 * tok + 8], CEYEB[:, :]),
                            waits=([("cDVE", max(mid["bld"], mid["bqkev"][pj]))]
                                   if t4 == 0 else []))
                        s.emit("tensor", lambda t, pj=pj, t4=t4, tok=tok: t.transpose(
                            PTB[pj][0:8, 256 * t4 + 128:256 * t4 + 256],
                            BPACK[:, 512 + 8 * tok:512 + 8 * tok + 8], CEYEB[:, :]),
                            inc=("cPE" if t4 == 3 else None))
                    mid["btr"][pj] = s.c("cPE")
                    if cupto < 3:
                        continue
                    s.emit("vector", lambda v, pj=pj: v.tensor_copy(
                        out=BQKT[0:8, 1024 * pj:1024 * pj + 1024],
                        in_=PTB[pj][0:8, 0:1024]),
                        waits=[("cPE", mid["btr"][pj]), ("cPE", mid["mm1"][pj]),
                               ("dST", st["store_done"])],
                        inc="cDVE")
                    mid["bqkev"][pj] = s.c("cDVE")
                    if cupto < 4:
                        continue
                    for t4 in range(4):
                        s.emit("tensor", lambda t, pj=pj, t4=t4: t.matmul(
                            PS[pj][:, 128 * t4:128 * t4 + 128],
                            BQKT[0:8, 1024 * pj + 256 * t4:1024 * pj + 256 * t4 + 128],
                            BQKT[0:8, 1024 * pj + 256 * t4 + 128:1024 * pj + 256 * t4 + 256],
                            start=True, stop=True),
                            waits=([("cDVE", max(mid["bqkev"][pj], mid["nmx"][pj])),
                                    ("cACT", mid["exp"][pj])] if t4 == 0 else []),
                            inc=("cPE" if t4 == 3 else None))
                    mid["mm1"][pj] = s.c("cPE")
                    if cupto < 5:
                        continue
                    s.emit("vector", lambda v, pj=pj: v.reduce_max(
                        out=NM[:, pj:pj + 1], in_=PS[pj][:, 0:512], axis=AX, negate=True),
                        waits=[("cPE", mid["mm1"][pj])], inc="cDVE")
                    mid["nmx"][pj] = s.c("cDVE")
                    if cupto < 6:
                        continue
                    s.emit("scalar", lambda sc, pj=pj: sc.activation(
                        out=EBUF[:, 512 * pj:512 * pj + 512], in_=PS[pj][:, 0:512],
                        func=AF.Exp, bias=NM[:, pj:pj + 1]),
                        waits=[("cDVE", mid["nmx"][pj]), ("cPE", mid["etr"][pj]),
                               ("dST", st["store_done"])],
                        inc="cACT")
                    mid["exp"][pj] = s.c("cACT")
                    if cupto < 7:
                        continue
                    for t4 in range(4):
                        s.emit("tensor", lambda t, pj=pj, t4=t4: t.transpose(
                            PTB[2 + pj][:, 128 * t4:128 * t4 + 128],
                            EBUF[:, 512 * pj + 128 * t4:512 * pj + 128 * t4 + 128],
                            CEYEB[:, :]),
                            waits=([("cACT", max(mid["exp"][pj], mid["etev"][pj]))]
                                   if t4 == 0 else []),
                            inc=("cPE" if t4 == 3 else None))
                    mid["etr"][pj] = s.c("cPE")
                    s.emit("scalar", lambda sc, pj=pj: sc.copy(
                        out=BET[:, 512 * pj:512 * pj + 512],
                        in_=PTB[2 + pj][:, 0:512]),
                        waits=[("cPE", max(mid["etr"][pj], mid["mm3"][pj])),
                               ("dST", st["store_done"])],
                        inc="cACT")
                    mid["etev"][pj] = s.c("cACT")
                    if cupto < 8:
                        continue
                    for t4 in range(4):
                        tl = (4 * j + t4) % 32
                        s.emit("tensor", lambda t, pj=pj, t4=t4, tl=tl, ob=ovbank, j=j: t.matmul(
                            PS[ob][:, 16 * tl:16 * tl + 9],
                            BET[:, 512 * pj + 128 * t4:512 * pj + 128 * t4 + 128],
                            BPACK[:, 1024 + 9 * (4 * j + t4):1024 + 9 * (4 * j + t4) + 9],
                            start=True, stop=True),
                            waits=([("cACT", mid["etev"][pj]),
                                    ("cDVE", mid["norm"][ovbank])] if t4 == 0 else []),
                            inc=("cPE" if t4 == 3 else None))
                    mid["mm3"][pj] = s.c("cPE")
                    if cupto < 9:
                        continue
                    if j % 8 == 7:
                        hb = j // 8
                        ob = 2 + hb
                        tok0 = w0 + 32 * hb
                        wmm3 = s.c("cPE")
                        s.emit("vector", lambda v, hb=hb, ob=ob: v.reciprocal(
                            out=RW[:, 32 * hb:32 * hb + 32], in_=PS[ob][:, 8:512:16]),
                            waits=[("cPE", wmm3)], inc="cDVE")
                        wrec = s.c("cDVE")
                        for g in range(8):
                            s.emit("vector", lambda v, g=g, hb=hb, ob=ob, tok0=tok0:
                                   v.tensor_tensor(
                                       out=QKVO[:, OVN0 + 8 * tok0 + g:
                                                OVN0 + 8 * tok0 + g + 8 * 31 + 1:8],
                                       in0=PS[ob][:, g:512:16],
                                       in1=RW[:, 32 * hb:32 * hb + 32], op=ALU.mult),
                                   waits=([("cDVE", wrec)] if g == 0 else []),
                                   inc=("cDVE" if g == 7 else None))
                        mid["norm"][ob] = s.c("cDVE")

        def phase_D(c):
            wnorm = max(mid["norm"][2], mid["norm"][3])
            for h in range(FT):
                g = h >> 2
                s.emit("vector", lambda v, h=h, g=g: v.tensor_add(
                    res_t(h), res_t(h), QKVO[:, OVN0 + g:OVN0 + g + 8 * (T - 1) + 1:8]),
                    waits=([("cDVE", wnorm), ("dST", st["store_done"])] if h == 0 else []),
                    inc=("cDVE" if h == FT - 1 else None))
            st["merge_done"] = s.c("cDVE")
            ln_stats(c, which=1)
            ln_apply1()
            st["ln1_done"] = s.c("cDVE")

        def ln_stats(c, which):
            wsrc = s.c("cDVE")
            wdpw = p1_pw if (c == 0 and which == 1) else 0
            sq_free = [0, 0]
            for h in range(FT):
                par = h % 2
                waits = []
                if h == 0:
                    waits += [("cDVE", wsrc)]
                    if wdpw:
                        waits += [("dPWA", wdpw)]
                if sq_free[par]:
                    waits += [("cPE", sq_free[par])]
                s.emit("vector", lambda v, h=h, par=par: v.tensor_mul(
                    MIDQ[:, 512 * par:512 * par + 512], res_t(h), res_t(h)),
                    waits=waits, inc="cDVE")
                wsq = s.c("cDVE")
                s.emit("tensor", lambda t, h=h: t.matmul(
                    PS[2][0:1, 0:T], CONESB[:, :], res_t(h),
                    start=(h == 0), stop=(h == FT - 1)),
                    waits=[("cDVE", wsq)] + ([("cACT", st["res_ready"])] if h == 0 else []))
                s.emit("tensor", lambda t, h=h, par=par: t.matmul(
                    PS[3][0:1, 0:T], CONESB[:, :], MIDQ[:, 512 * par:512 * par + 512],
                    start=(h == 0), stop=(h == FT - 1)),
                    inc="cPE")
                sq_free[par] = s.c("cPE")
            wstat = s.c("cPE")
            steps = [
                lambda v: v.tensor_scalar_mul(SVEC[0:1, 0:512], PS[2][0:1, 0:T], 1.0 / D),
                lambda v: v.tensor_scalar_mul(SVEC[0:1, 512:1024], PS[3][0:1, 0:T], 1.0 / D),
                lambda v: v.tensor_mul(SVEC[0:1, 1024:1536], SVEC[0:1, 0:512], SVEC[0:1, 0:512]),
                lambda v: v.tensor_sub(SVEC[0:1, 1536:2048], SVEC[0:1, 512:1024], SVEC[0:1, 1024:1536]),
                lambda v: v.tensor_scalar_add(SVEC[0:1, 2048:2560], SVEC[0:1, 1536:2048], EPS),
                lambda v: v.reciprocal(SVEC[0:1, 2560:3072], SVEC[0:1, 2048:2560]),
            ]
            wprev = wstat
            for i, fstep in enumerate(steps):
                s.emit("vector", fstep,
                       waits=([("cPE", wstat)] if i == 0 else [("cDVE", wprev)]),
                       inc="cDVE")
                wprev = s.c("cDVE")
            s.emit("scalar", lambda sc: sc.activation(
                out=SVEC[0:1, 1024:1536], in_=SVEC[0:1, 2560:3072], func=AF.Sqrt),
                waits=[("cDVE", wprev)], inc="cACT")
            wrstd = s.c("cACT")
            s.emit("tensor", lambda t: t.matmul(
                PS[2][:, 0:T], CONESF[:, :], SVEC[0:1, 0:512], start=True, stop=True),
                waits=[("cACT", wrstd)])
            s.emit("tensor", lambda t: t.matmul(
                PS[3][:, 0:T], CONESF[:, :], SVEC[0:1, 1024:1536], start=True, stop=True),
                inc="cPE")
            return s.c("cPE")

        def ln_apply1():
            wbc = s.c("cPE")
            for h in range(FT):
                par = h % 2
                s.emit("vector", lambda v, h=h, par=par: v.tensor_sub(
                    GSIL[:, 512 * par:512 * par + 512], res_t(h), PS[2][:, 0:T]),
                    waits=([("cPE", wbc), ("dST", st["store_done"])]
                           if h == 0 else []), inc="cDVE")
                s.emit("vector", lambda v, h=h, par=par: v.tensor_mul(
                    res_t(h), GSIL[:, 512 * par:512 * par + 512], PS[3][:, 0:T]),
                    inc="cDVE")

        def phase_q_gu(c, q):
            n, s0 = QN[q], QS0[q]
            blocks = []
            b0 = 0
            while b0 < n:
                blocks.append((b0, min(2, n - b0)))
                b0 += 2
            wld = {}
            gu_mm_done = {}
            mm_done = []

            def emit_loads(i):
                blk0, bn = blocks[i]
                par = i % 2
                sl0 = s0 + blk0
                for mat, mTb, buf, wfsem in (("g", wgT_b, WBUF, f"dWFg{par}"),
                                             ("u", wuT_b, QKVO, f"dWFu{par}")):
                    need = gT_done[(mat, q)]
                    wfree_pe = mm_done[i - 2] if i >= 2 else (
                        st["wbuf_free"] if mat == "g" else st["qkvo_free_pe"])
                    extra = []
                    if mat == "u" and i < 2:
                        extra = [("cDVE", st["ln1_done"]), ("dST", st["store_done"])]
                    s.emit("sync", lambda sy, mTb=mTb, buf=buf, sl0=sl0, bn=bn, par=par:
                           sy.dma_start(
                               out=buf[:, 8192 * par:8192 * par + 256 * FT]
                               .rearrange("p (hh j) -> p hh j", j=256)[:, :, 0:128 * bn],
                               in_=mTb[:, P * sl0:P * sl0 + P * bn]
                               .rearrange("(hh p) j -> p hh j", p=P)),
                           waits=list(need) + [("cPE", wfree_pe)] + extra,
                           inc=wfsem, amt=16, dma=True)
                    wld[(mat, i)] = s.c(wfsem)

            def emit_compute(i):
                blk0, bn = blocks[i]
                par2 = i % 2
                sl0 = s0 + blk0
                for bi in range(bn):
                    sl = sl0 + bi
                    par = sl % 2
                    for h in range(FT):
                        waits = [(f"dWFg{par2}", wld[("g", i)])]
                        if h == 0:
                            waits += [("cDVE", max(st["ln1_done"], st["dn_free"])),
                                      ("cACT", st["sil_free"][par])]
                        s.emit("tensor", lambda t, h=h, bi=bi, par=par, par2=par2: t.matmul(
                            PS[par][:, 0:T],
                            WBUF[:, 8192 * par2 + 256 * h + 128 * bi:
                                 8192 * par2 + 256 * h + 128 * bi + 128],
                            res_t(h), start=(h == 0), stop=(h == FT - 1)),
                            waits=waits, inc=("cPE" if h == FT - 1 else None))
                    wgmm = s.c("cPE")
                    s.emit("scalar", lambda sc, par=par: sc.activation(
                        out=GSIL[:, 512 * par:512 * par + 512], in_=PS[par][:, 0:T],
                        func=AF.Silu, scale=CSCL[:, 0:1]),
                        waits=[("cPE", wgmm), ("cDVE", st["mid_free"][par])], inc="cACT")
                    st["sil_free"][par] = s.c("cACT")
                    wsil = s.c("cACT")
                    for h in range(FT):
                        waits = [(f"dWFu{par2}", wld[("u", i)])]
                        if h == 0:
                            waits += [("cACT", wsil),
                                      ("cDVE", max(st["mid_free"][par], st["dn_free"]))]
                        s.emit("tensor", lambda t, h=h, bi=bi, par=par, par2=par2: t.matmul(
                            PS[2 + par][:, 0:T],
                            QKVO[:, 8192 * par2 + 256 * h + 128 * bi:
                                 8192 * par2 + 256 * h + 128 * bi + 128],
                            res_t(h), start=(h == 0), stop=(h == FT - 1)),
                            waits=waits, inc=("cPE" if h == FT - 1 else None))
                    wumm = s.c("cPE")
                    mq = sl - s0
                    s.emit("vector", lambda v, mq=mq, par=par: v.scalar_tensor_tensor(
                        out=MIDQ[:, 512 * mq:512 * mq + 512], in0=PS[2 + par][:, 0:T],
                        scalar=CSCL[:, 1:2], in1=GSIL[:, 512 * par:512 * par + 512],
                        op0=ALU.mult, op1=ALU.mult),
                        waits=[("cPE", wumm)], inc="cDVE")
                    st["mid_free"][par] = s.c("cDVE")
                mm_done.append(s.c("cPE"))

            for i in range(len(blocks)):
                emit_loads(i)
                if i >= 1:
                    emit_compute(i - 1)
            emit_compute(len(blocks) - 1)
            st["wbuf_free"] = s.c("cPE")
            st["qkvo_free_pe"] = s.c("cPE")
            return s.c("cDVE")

        def phase_q_down(c, q, wmid):
            n, s0 = QN[q], QS0[q]
            for fg in range(8):
                sub = []
                for sb0 in range(0, n, 8):
                    sub.append((sb0, min(8, n - sb0)))
                for (sb0, sbn) in sub:
                    par8 = (sb0 // 8) % 2
                    slot = WDN[:, 4096 * par8:4096 * par8 + 512 * sbn]
                    need = dT_done[s0 + sb0 + sbn - 1]
                    wds = "dWD0" if par8 == 0 else "dWD1"
                    s.emit("sync", lambda sy, s0=s0, sb0=sb0, sbn=sbn, fg=fg, sl_=slot:
                           sy.dma_start(
                               out=sl_.rearrange("p (ss j) -> p ss j", j=512),
                               in_=wdT_b[P * (s0 + sb0):P * (s0 + sb0) + P * sbn,
                                         512 * fg:512 * fg + 512]
                               .rearrange("(ss p) j -> p ss j", p=P)),
                           waits=list(need) + [("cPE", st["wdn_free"][par8]),
                                  ("dST", st["store_done"])],
                           inc=wds, amt=16, dma=True)
                    wldd = s.c(wds)
                    for si in range(sbn):
                        sl = s0 + sb0 + si
                        first = (sb0 + si == 0)
                        last = (sb0 + si == n - 1)
                        for tf in range(4):
                            waits = [(wds, wldd)]
                            if first and tf == 0:
                                waits += [("cDVE", max(wmid, st["dn_free"]))]
                                if c == 0 and q == 0 and fg == 0:
                                    waits += p3_done[0] + p3_done[1]
                            s.emit("tensor", lambda t, si=si, tf=tf, sl=sl, par8=par8,
                                   s0=s0, first=first, last=last: t.matmul(
                                       PS[tf][:, 0:T],
                                       WDN[:, 4096 * par8 + 512 * si + 128 * tf:
                                           4096 * par8 + 512 * si + 128 * tf + 128],
                                       MIDQ[:, 512 * (sl - s0):512 * (sl - s0) + 512],
                                       start=first, stop=last),
                                   waits=waits,
                                   inc=("cPE" if tf == 3 and si == sbn - 1 else None))
                    st["wdn_free"][par8] = s.c("cPE")
                wdn = s.c("cPE")
                for tf in range(4):
                    h = 4 * fg + tf
                    pslice = PART[:, T * h:T * h + T]
                    if q == 0:
                        s.emit("vector", lambda v, tf=tf, ps=pslice: v.tensor_scalar(
                            out=ps, in0=PS[tf][:, 0:T], scalar1=CSCL[:, 2:3],
                            scalar2=None, op0=ALU.mult),
                            waits=([("cPE", wdn)] if tf == 0 else []),
                            inc=("cDVE" if tf == 3 else None))
                    elif q < 3:
                        s.emit("vector", lambda v, tf=tf, ps=pslice: v.scalar_tensor_tensor(
                            out=ps, in0=PS[tf][:, 0:T], scalar=CSCL[:, 2:3], in1=ps,
                            op0=ALU.mult, op1=ALU.add),
                            waits=([("cPE", wdn)] if tf == 0 else []),
                            inc=("cDVE" if tf == 3 else None))
                    else:
                        s.emit("vector", lambda v, tf=tf, ps=pslice: v.scalar_tensor_tensor(
                            out=GSIL[:, 0:512], in0=PS[tf][:, 0:T], scalar=CSCL[:, 2:3],
                            in1=ps, op0=ALU.mult, op1=ALU.add),
                            waits=([("cPE", wdn), ("dST", st["store_done"])]
                                   if tf == 0 else []), inc="cDVE")
                        wtmp = s.c("cDVE")
                        s.emit("vector", lambda v, h=h: v.tensor_add(
                            res_t(h), res_t(h), GSIL[:, 0:512]),
                            waits=[("cDVE", wtmp)], inc="cDVE")
                st["dn_free"] = s.c("cDVE")

        def phase_out(c):
            wbc = ln_stats(c, which=2)
            ev_free = [0, 0]
            prev_store = 0
            for half in range(2):
                for h in range(FT):
                    waits = []
                    if h == 0:
                        waits += [("cPE", wbc)]
                        if half == 1:
                            waits += [("cPE", st["out_tr"])]
                    s.emit("vector", lambda v, h=h, half=half: v.tensor_sub(
                        GSIL[:, 512:768], res_t(h)[:, 256 * half:256 * half + 256],
                        PS[2][:, 256 * half:256 * half + 256]),
                        waits=waits, inc="cDVE")
                    s.emit("vector", lambda v, h=h, half=half: v.tensor_mul(
                        h2f[:, 256 * h:256 * h + 256], GSIL[:, 512:768],
                        PS[3][:, 256 * half:256 * half + 256]),
                        inc="cDVE")
                wapp = s.c("cDVE")
                for tt in (2 * half, 2 * half + 1):
                    for h in range(FT):
                        bank = PS[h % 2]
                        waits = [("cACT", ev_free[h % 2])] if ev_free[h % 2] else []
                        if h == 0:
                            waits += [("cDVE", wapp)]
                            if prev_store:
                                waits += [("dST", prev_store)]
                        s.emit("tensor", lambda t, h=h, tt=tt, b=bank: t.transpose(
                            b[:, 0:P],
                            h2f[:, 256 * h + 128 * (tt % 2):256 * h + 128 * (tt % 2) + 128],
                            CEYEF[:, :]),
                            waits=waits, inc="cPE")
                        wtr = s.c("cPE")
                        s.emit("scalar", lambda sc, h=h, b=bank: sc.copy(
                            out=ostf[:, h:4096:32], in_=b[:, 0:P]),
                            waits=[("cPE", wtr)], inc="cACT")
                        ev_free[h % 2] = s.c("cACT")
                    wev = s.c("cACT")
                    s.emit("sync", lambda sy, c=c, tt=tt: sy.dma_start(
                        out=y_out[T * c + P * tt:T * c + P * tt + P, :], in_=ostf[:, :]),
                        waits=[("cACT", wev)], inc="dST", amt=16, dma=True)
                    prev_store = s.c("dST")
                st["out_tr"] = s.c("cPE")
            st["store_done"] = s.c("dST")

        def dbg_dump(name, src_ap, waits):
            if dbg:
                s.emit("sync", lambda sy, d=dbg_t[name], sp=src_ap: sy.dma_start(
                    out=d[:, 0:sp.shape[1]], in_=sp), waits=waits,
                    inc="dST", amt=16, dma=True)
                st["store_done"] = s.c("dST")

        p3_done = [[], []]
        for c in range(nchunk):
            if upto < 1:
                break
            phase_A(c)
            if c == 0:
                dbg_dump("d_x", RES[:, :], [("cACT", st["res_ready"])])
            if upto < 2:
                break
            phase_B(c)
            if c == 0:
                dbg_dump("d_qkv", QKVO[:, 0:12288], [("cDVE", st["qkv_evac"])])
            if upto < 3:
                break
            phase_C(c)
            if c == 0:
                dbg_dump("d_ovn", QKVO[:, 12288:16384],
                         [("cDVE", max(mid["norm"][2], mid["norm"][3]))])
            if upto < 4:
                break
            phase_D(c)
            if c == 0:
                p3_done[0], p3_done[1] = emit_P3()
                dbg_dump("d_h1", RES[:, :], [("cDVE", st["ln1_done"])])
            if upto < 5:
                break
            for q in range(4):
                wmid = phase_q_gu(c, q)
                if upto < 6:
                    break
                phase_q_down(c, q, wmid)
            if c == 0:
                dbg_dump("d_h2", RES[:, :], [("cDVE", st["dn_free"])])
            if upto < 7:
                break
            phase_out(c)

        final_st = s.c("dST")

        def replay(eng_obj, name):
            last = {}
            for ws, fn, inc, amt in s.ops[name]:
                for sm, v in ws:
                    if v > last.get(sm, -1):
                        eng_obj.wait_ge(sems[sm], v)
                        last[sm] = v
                ins = fn(eng_obj)
                if inc is not None:
                    ins.then_inc(sems[inc], amt)
            if name == "sync":
                eng_obj.wait_ge(sems["dST"], final_st)

        @block.sync
        def _(sync):
            replay(sync, "sync")

        @block.scalar
        def _(scalar):
            replay(scalar, "scalar")

        @block.vector
        def _(vector):
            replay(vector, "vector")

        @block.tensor
        def _(tensor):
            replay(tensor, "tensor")

        @block.gpsimd
        def _(gpsimd):
            replay(gpsimd, "gpsimd")

    return nc


def kernel(L0_attn_x, L0_attn_Wq, L0_attn_bq, L0_attn_Wk, L0_attn_bk,
           L0_attn_Wv, L0_attn_bv, L0_ffn_w_gate, L0_ffn_w_up, L0_ffn_w_down,
           L0_ffn_scale_gate, L0_ffn_scale_up, L0_ffn_scale_down):
    import ml_dtypes
    x = np.ascontiguousarray(np.asarray(L0_attn_x, dtype=np.float32).reshape(16384, D))
    wq_h = np.ascontiguousarray(np.asarray(L0_attn_Wq, np.float32).reshape(D, 4096))
    wk_h = np.ascontiguousarray(np.asarray(L0_attn_Wk, np.float32).reshape(D, 1024))
    wv_h = np.ascontiguousarray(np.asarray(L0_attn_Wv, np.float32).reshape(D, 1024))
    bq_h = np.ascontiguousarray(np.asarray(L0_attn_bq, np.float32))
    bk_h = np.ascontiguousarray(np.asarray(L0_attn_bk, np.float32))
    bv_h = np.ascontiguousarray(np.asarray(L0_attn_bv, np.float32))
    wg_h = np.ascontiguousarray(np.asarray(L0_ffn_w_gate, np.int32))
    wu_h = np.ascontiguousarray(np.asarray(L0_ffn_w_up, np.int32))
    wd_h = np.ascontiguousarray(np.asarray(L0_ffn_w_down, np.int32))
    scl_h = np.ascontiguousarray(np.broadcast_to(
        np.stack([np.asarray(L0_ffn_scale_gate, np.float32).reshape(()),
                  np.asarray(L0_ffn_scale_up, np.float32).reshape(()),
                  np.asarray(L0_ffn_scale_down, np.float32).reshape(())]).reshape(1, 3),
        (P, 3)))
    eyef_h = np.eye(P, dtype=np.float32)
    eyeb_h = np.eye(P).astype(ml_dtypes.bfloat16)
    onesb_h = np.ones((P, 1)).astype(ml_dtypes.bfloat16)
    onesf_h = np.ones((1, P), dtype=np.float32)

    nc = build_nc()
    in_maps = []
    for cid in range(NCORE):
        in_maps.append({
            "x_in": np.ascontiguousarray(x[TOK_CORE * cid: TOK_CORE * (cid + 1)]),
            "wq": wq_h, "wk": wk_h, "wv": wv_h,
            "bq": bq_h, "bk": bk_h, "bv": bv_h,
            "wg": wg_h, "wu": wu_h, "wd": wd_h,
            "scl": scl_h, "eyef": eyef_h, "eyeb": eyeb_h,
            "onesb": onesb_h, "onesf": onesf_h,
        })
    res = run_bass_kernel_spmd(nc, in_maps, core_ids=list(range(NCORE)))
    out = np.concatenate([r["y_out"] for r in res.results], axis=0)
    return out.reshape(2, 8192, D)
